# version 20
# baseline (speedup 1.0000x reference)
import math
from contextlib import ExitStack

import numpy as np

import concourse.bass as bass
import concourse.mybir as mybir
from concourse.bass_utils import run_bass_kernel_spmd

F32 = mybir.dt.float32
BF16 = mybir.dt.bfloat16
AF = mybir.ActivationFunctionType
ALU = mybir.AluOpType
AX = mybir.AxisListType.X

S = 2048
D = 2048
DFF = 8192
NH = 32
NSEL = 256
CEXP = 30.0
EPS = 1e-5
NT = S // 128
NC = D // 128
NFT = DFF // 128
NPAIR = NH // 2

DEV_TAPS = ()
PHASE_LIMIT = 99


def _bucket(d):
    rp = max(d, 0)
    if rp < 16:
        return rp
    b = 16 + int(math.log(rp / 16.0) / math.log(8.0) * 16.0)
    return min(b, 31)


def host_consts(rel_bias, Wr, br, b1, b2):
    rel = np.asarray(rel_bias, np.float32)
    bidx = np.empty((128, 256), np.int64)
    for p in range(128):
        for c in range(256):
            bidx[p, c] = _bucket(c - p)
    traw = np.ascontiguousarray(rel[bidx].transpose(2, 0, 1))
    return {
        "traw": traw,
        "v0r": np.ascontiguousarray(np.broadcast_to(rel[0], (128, NH))),
        "v31r": np.ascontiguousarray(np.broadcast_to(rel[31], (128, NH))),
        "ident": np.eye(128, dtype=np.float32),
        "iota_r": np.arange(256, dtype=np.float32).reshape(1, 256),
        "iota_c": np.arange(128, dtype=np.float32).reshape(128, 1),
        "eyef16": np.ascontiguousarray(np.eye(16, dtype=np.float32).reshape(1, 256)),
        "ones_r": np.ones((1, 128), np.float32),
        "wr_row": np.ascontiguousarray(np.asarray(Wr, np.float32).reshape(1, D)),
        "br_in": np.asarray(br, np.float32).reshape(1, 1),
        "b1_in": np.ascontiguousarray(np.asarray(b1, np.float32).reshape(DFF // 128, 128).T),
        "b2_in": np.ascontiguousarray(np.asarray(b2, np.float32).reshape(1, D)),
    }


def build(nc):
    es = ExitStack()
    dev = set(DEV_TAPS)

    x_h = nc.dram_tensor("x", [S, D], F32, kind="ExternalInput").ap()
    wq_h = nc.dram_tensor("Wq", [D, D], F32, kind="ExternalInput").ap()
    wk_h = nc.dram_tensor("Wk", [D, D], F32, kind="ExternalInput").ap()
    wv_h = nc.dram_tensor("Wv", [D, D], F32, kind="ExternalInput").ap()
    wo_h = nc.dram_tensor("Wo", [D, D], F32, kind="ExternalInput").ap()
    w1_h = nc.dram_tensor("W1", [D, DFF], F32, kind="ExternalInput").ap()
    w2_h = nc.dram_tensor("W2", [DFF, D], F32, kind="ExternalInput").ap()
    traw_h = nc.dram_tensor("traw", [NH, 128, 256], F32, kind="ExternalInput").ap()
    v0r_h = nc.dram_tensor("v0r", [128, NH], F32, kind="ExternalInput").ap()
    v31r_h = nc.dram_tensor("v31r", [128, NH], F32, kind="ExternalInput").ap()
    ident_h = nc.dram_tensor("ident", [128, 128], F32, kind="ExternalInput").ap()
    iota_r_h = nc.dram_tensor("iota_r", [1, 256], F32, kind="ExternalInput").ap()
    iota_c_h = nc.dram_tensor("iota_c", [128, 1], F32, kind="ExternalInput").ap()
    eyef16_h = nc.dram_tensor("eyef16", [1, 256], F32, kind="ExternalInput").ap()
    ones_r_h = nc.dram_tensor("ones_r", [1, 128], F32, kind="ExternalInput").ap()
    wr_row_h = nc.dram_tensor("wr_row", [1, D], F32, kind="ExternalInput").ap()
    br_h = nc.dram_tensor("br_in", [1, 1], F32, kind="ExternalInput").ap()
    b1_h = nc.dram_tensor("b1_in", [128, DFF // 128], F32, kind="ExternalInput").ap()
    b2_h = nc.dram_tensor("b2_in", [1, D], F32, kind="ExternalInput").ap()
    out_h = nc.dram_tensor("out", [S, D], F32, kind="ExternalOutput").ap()

    spill_kind = "ExternalOutput" if "spills" in dev else "Internal"
    xn_sp = nc.dram_tensor("xn_sp", [S, D], BF16, kind=spill_kind).ap()
    ot_sp = nc.dram_tensor("ot_sp", [D, S], BF16, kind=spill_kind).ap()
    taps = {}
    for name, shape, dt in (
        ("t_ltm", [128, NT], F32), ("t_mask", [128, NT], F32),
        ("t_pref", [128, NT], F32), ("t_wrow", [1, S], F32),
        ("t_seltm", [128, NT * 256], BF16), ("t_selwt", [128, 2 * S], BF16),
        ("t_outl", [128, 2 * D], BF16), ("t_stat", [128, NT * 8], F32),
    ):
        if name in dev:
            taps[name] = nc.dram_tensor(name, shape, dt, kind="ExternalOutput").ap()

    plan = {k: [] for k in ("T", "V", "Sc", "SY", "G")}

    def mksem(name):
        return {"h": es.enter_context(nc.semaphore(name)), "n": 0}

    tS, vS, sS = (mksem(n) for n in ("tS", "vS", "sS"))

    class DQ:
        def __init__(self, sems):
            self.sems = sems
            self.i = 0
            self.log = []

        def __getitem__(self, k):
            assert k == "n"
            return 16 * len(self.log)

    dS = DQ([mksem(f"dS{i}") for i in range(8)])
    dG = DQ([mksem(f"dG{i}") for i in range(4)])

    def w(st, sem, tgt=None):
        if isinstance(sem, DQ):
            k = len(sem.log) if tgt is None else (tgt + 15) // 16
            last = {}
            for s, cnt in sem.log[:k]:
                last[id(s)] = (s, cnt)
            for s, cnt in last.values():
                plan[st].append(("w", s, cnt))
            return
        plan[st].append(("w", sem, sem["n"] if tgt is None else tgt))

    OWN = {"V": vS, "Sc": sS}

    def e(st, fn, *incs):
        realincs = []
        own = OWN.get(st)
        if own is not None:
            plan[st].append(("w", own, own["n"]))
            own["n"] += 1
            realincs.append((own, 1))
        for sem, amt in incs:
            if isinstance(sem, DQ):
                s = sem.sems[sem.i]
                sem.i = (sem.i + 1) % len(sem.sems)
                plan[st].append(("w", s, s["n"]))
                sem.log.append((s, s["n"] + 16))
                s["n"] += 16
                realincs.append((s, 16))
            else:
                sem["n"] += amt
                realincs.append((sem, amt))
        merged = {}
        for s_, a_ in realincs:
            k_ = id(s_)
            if k_ in merged:
                merged[k_] = (s_, merged[k_][1] + a_)
            else:
                merged[k_] = (s_, a_)
        plan[st].append(("e", fn, list(merged.values())))

    ENG = {"T": lambda: nc.tensor, "V": lambda: nc.vector, "Sc": lambda: nc.scalar}

    def barrier():
        marks = {"T": tS["n"], "V": vS["n"], "Sc": sS["n"]}
        for st in ("T", "V", "Sc", "SY", "G"):
            for st2 in ("T", "V", "Sc"):
                if st2 != st:
                    w(st, {"T": tS, "V": vS, "Sc": sS}[st2], marks[st2])
            w(st, dS)
            w(st, dG)

    def finish():
        def runner(st):
            def go(eng):
                for item in plan[st]:
                    if item[0] == "w":
                        eng.wait_ge(item[1]["h"], item[2])
                    else:
                        inst = item[1](eng)
                        for sem, amt in item[2]:
                            inst = inst.then_inc(sem["h"], amt)
            return go
        with nc.Block() as block:
            block.tensor(runner("T"))
            block.vector(runner("V"))
            block.scalar(runner("Sc"))
            block.sync(runner("SY"))
            block.gpsimd(runner("G"))
        es.close()

    selwt = es.enter_context(nc.sbuf_tensor("selwt", [128, 2 * S], BF16))
    outl = es.enter_context(nc.sbuf_tensor("outl", [128, 2 * D], BF16))
    ident_bf = es.enter_context(nc.sbuf_tensor("ident_bf", [128, 128], BF16))
    ident_f = es.enter_context(nc.sbuf_tensor("ident_f", [128, 128], F32))
    v0c = es.enter_context(nc.sbuf_tensor("v0c", [128, NH], F32))
    v31c = es.enter_context(nc.sbuf_tensor("v31c", [128, NH], F32))
    onesr_f = es.enter_context(nc.sbuf_tensor("onesr_f", [1, 128], F32))
    onesr_b = es.enter_context(nc.sbuf_tensor("onesr_b", [1, 128], BF16))

    e("G", lambda g: g.dma_start(out=ident_bf[:], in_=ident_h), (dG, 16))
    e("G", lambda g: g.dma_start(out=onesr_b[:], in_=ones_r_h), (dG, 16))
    e("SY", lambda g: g.dma_start(out=ident_f[:], in_=ident_h), (dS, 16))
    e("SY", lambda g: g.dma_start(out=v0c[:], in_=v0r_h), (dS, 16))
    e("SY", lambda g: g.dma_start(out=v31c[:], in_=v31r_h), (dS, 16))
    e("SY", lambda g: g.dma_start(out=onesr_f[:], in_=ones_r_h), (dS, 16))
    w("V", dS)
    e("V", lambda g: nc.vector.tensor_scalar_add(v0c[:], v0c[:], -CEXP))
    e("V", lambda g: nc.vector.tensor_scalar_add(v31c[:], v31c[:], -CEXP), (vS, 1))

    aA = ExitStack()
    sel_tm = aA.enter_context(nc.sbuf_tensor("sel_tm", [128, NT * 256], BF16))
    iotar = aA.enter_context(nc.sbuf_tensor("iotar", [128, 256], F32))
    iotac = aA.enter_context(nc.sbuf_tensor("iotac", [128, 1], F32))
    eyef16 = aA.enter_context(nc.sbuf_tensor("eyef16_s", [1, 256], F32))
    b1_sb = aA.enter_context(nc.sbuf_tensor("b1_sb", [128, NFT], F32))
    brr = aA.enter_context(nc.sbuf_tensor("brr", [128, 1], F32))
    ltm = aA.enter_context(nc.sbuf_tensor("ltm", [128, NT], F32))
    rank_tm = aA.enter_context(nc.sbuf_tensor("rank_tm", [128, NT], F32))
    mask_tm = aA.enter_context(nc.sbuf_tensor("mask_tm", [128, NT], F32))
    pref_tm = aA.enter_context(nc.sbuf_tensor("pref_tm", [128, NT], F32))
    lrow = aA.enter_context(nc.sbuf_tensor("lrow", [1, S], F32))
    mrow = aA.enter_context(nc.sbuf_tensor("mrow", [1, S], F32))
    prow = aA.enter_context(nc.sbuf_tensor("prow", [1, S], F32))
    wrow = aA.enter_context(nc.sbuf_tensor("wrow", [1, S], F32))
    sm1 = aA.enter_context(nc.sbuf_tensor("sm1", [1, 8], F32))
    scrb = aA.enter_context(nc.sbuf_tensor("scrb", [128, D], BF16))

    e("SY", lambda g: g.dma_start(out=iotar[:], in_=iota_r_h.partition_broadcast(128)), (dS, 16))
    e("SY", lambda g: g.dma_start(out=iotac[:], in_=iota_c_h), (dS, 16))
    e("SY", lambda g: g.dma_start(out=eyef16[:], in_=eyef16_h), (dS, 16))
    e("SY", lambda g: g.dma_start(out=b1_sb[:], in_=b1_h), (dS, 16))
    e("SY", lambda g: g.dma_start(out=brr[:], in_=br_h.partition_broadcast(128)), (dS, 16))

    a1 = ExitStack()
    xbuf = [a1.enter_context(nc.sbuf_tensor(f"xbuf{i}", [128, D], F32)) for i in range(2)]
    xnb = [a1.enter_context(nc.sbuf_tensor(f"xnb{i}", [128, D], BF16)) for i in range(2)]
    xnf = a1.enter_context(nc.sbuf_tensor("xnf", [128, D], F32))
    wrr = a1.enter_context(nc.sbuf_tensor("wrr", [128, D], F32))
    stat = a1.enter_context(nc.sbuf_tensor("stat", [128, 8], F32))
    statall = a1.enter_context(nc.sbuf_tensor("statall", [128, NT * 8], F32)) if "t_stat" in taps else None

    e("SY", lambda g: g.dma_start(out=wrr[:], in_=wr_row_h.partition_broadcast(128)), (dS, 16))
    for tt in range(NT):
        b = tt % 2
        if tt >= 2:
            w("SY", vS)
        e("SY", (lambda tt=tt, b=b: lambda g: g.dma_start(
            out=xbuf[b][:], in_=x_h[tt * 128:(tt + 1) * 128, :]))(), (dS, 16))
        w("V", dS)
        e("V", (lambda b=b: lambda g: nc.vector.reduce_sum(stat[:, 4:5], xbuf[b][:], axis=AX))())
        e("V", lambda g: nc.vector.tensor_scalar_mul(stat[:, 0:1], stat[:, 4:5], -1.0 / D), (vS, 1))
        w("Sc", vS)
        e("Sc", (lambda b=b: lambda g: nc.scalar.activation(
            out=scrb[:], in_=xbuf[b][:], func=AF.Square, bias=stat[:, 0:1], scale=1.0,
            accum_out=stat[:, 1:2]))(), (sS, 1))
        w("V", sS)
        e("V", lambda g: nc.vector.tensor_scalar(
            out=stat[:, 2:3], in0=stat[:, 1:2], scalar1=1.0 / D, scalar2=EPS,
            op0=ALU.mult, op1=ALU.add))
        e("V", lambda g: nc.vector.reciprocal(stat[:, 4:5], stat[:, 2:3]), (vS, 1))
        w("Sc", vS)
        e("Sc", lambda g: nc.scalar.activation(
            out=stat[:, 5:6], in_=stat[:, 4:5], func=AF.Sqrt), (sS, 1))
        w("V", sS)
        e("V", lambda g: nc.vector.tensor_mul(stat[:, 6:7], stat[:, 5:6], stat[:, 5:6]))
        e("V", lambda g: nc.vector.tensor_mul(stat[:, 6:7], stat[:, 6:7], stat[:, 2:3]))
        e("V", lambda g: nc.vector.tensor_scalar(
            out=stat[:, 6:7], in0=stat[:, 6:7], scalar1=-0.5, scalar2=1.5,
            op0=ALU.mult, op1=ALU.add))
        e("V", lambda g: nc.vector.tensor_mul(stat[:, 3:4], stat[:, 5:6], stat[:, 6:7]))
        e("V", (lambda b=b: lambda g: nc.vector.tensor_scalar(
            out=xnf[:], in0=xbuf[b][:], scalar1=stat[:, 0:1], scalar2=stat[:, 3:4],
            op0=ALU.add, op1=ALU.mult))())
        e("V", (lambda b=b: lambda g: nc.vector.tensor_copy(out=xnb[b][:], in_=xnf[:]))())
        e("V", (lambda tt=tt: lambda g: nc.vector.scalar_tensor_tensor(
            out=scrb[:], in0=xnf[:], scalar=1.0, in1=wrr[:], op0=ALU.mult, op1=ALU.mult,
            accum_out=ltm[:, tt:tt + 1]))(), (vS, 1))
        if statall is not None:
            e("V", (lambda tt=tt: lambda g: nc.vector.tensor_copy(
                out=statall[:, tt * 8:(tt + 1) * 8], in_=stat[:]))(), (vS, 1))
        w("SY", vS)
        e("SY", (lambda tt=tt, b=b: lambda g: g.dma_start(
            out=xn_sp[tt * 128:(tt + 1) * 128, :], in_=xnb[b][:]))(), (dS, 16))
    xn_spill_tgt = dS["n"]
    e("V", lambda g: nc.vector.tensor_scalar(
        out=ltm[:], in0=ltm[:], scalar1=brr[:, 0:1], scalar2=None, op0=ALU.add), (vS, 1))
    if "t_ltm" in taps:
        w("SY", vS)
        e("SY", lambda g: g.dma_start(out=taps["t_ltm"], in_=ltm[:]), (dS, 16))
    if statall is not None:
        w("SY", vS)
        e("SY", lambda g: g.dma_start(out=taps["t_stat"], in_=statall[:]), (dS, 16))
    if PHASE_LIMIT < 2:
        w("SY", dS)
        a1.close()
        aA.close()
        finish()
        return

    a2 = ExitStack()
    lrep = a2.enter_context(nc.sbuf_tensor("lrep", [128, S], F32))
    prep = a2.enter_context(nc.sbuf_tensor("prep", [128, S], F32))
    wrep = a2.enter_context(nc.sbuf_tensor("wrep", [128, S], F32))
    scan0 = a2.enter_context(nc.sbuf_tensor("scan0", [1, 3072], F32))
    scan1 = a2.enter_context(nc.sbuf_tensor("scan1", [1, 3072], F32))
    p2 = ExitStack()
    ps_big = p2.enter_context(nc.psum_tensor("ps_big", [128, S], F32))
    ps_tm16 = p2.enter_context(nc.psum_tensor("ps_tm16", [128, NT], F32))

    def rows_from_tm(col_fn, dst_row):
        w("T", vS)
        for tt in range(NT):
            e("T", (lambda tt=tt: lambda g: nc.tensor.matmul(
                ps_big[0:1, tt * 128:(tt + 1) * 128], col_fn(tt), ident_f[:],
                start=True, stop=True))(), *(((tS, 1),) if tt == NT - 1 else ()))
        w("V", tS)
        e("V", (lambda d_=dst_row: lambda g: nc.vector.tensor_copy(
            out=d_[0:1, :], in_=ps_big[0:1, :]))(), (vS, 1))

    def replicate(row_sb, dst_rep):
        w("T", vS)
        for j in range(4):
            e("T", (lambda j=j, r=row_sb: lambda g: nc.tensor.matmul(
                ps_big[:, j * 512:(j + 1) * 512], onesr_f[0:1, :],
                r[0:1, j * 512:(j + 1) * 512], start=True, stop=True))(),
              *(((tS, 1),) if j == 3 else ()))
        w("V", tS)
        e("V", (lambda d_=dst_rep: lambda g: nc.vector.tensor_copy(
            out=d_[:], in_=ps_big[:]))(), (vS, 1))

    w("T", dS)
    rows_from_tm(lambda tt: ltm[:, tt:tt + 1], lrow)
    replicate(lrow, lrep)
    if PHASE_LIMIT < 1.2:
        w("SY", dS)
        p2.close()
        a2.close()
        a1.close()
        aA.close()
        finish()
        return
    w("V", vS)
    for tt in range(NT):
        e("V", (lambda tt=tt: lambda g: nc.vector.tensor_scalar(
            out=scrb[:, 0:S], in0=lrep[:], scalar1=ltm[:, tt:tt + 1], scalar2=None,
            op0=ALU.is_gt, op1=ALU.add, accum_out=rank_tm[:, tt:tt + 1]))())
    e("V", lambda g: nc.vector.tensor_scalar(
        out=mask_tm[:], in0=rank_tm[:], scalar1=float(NSEL), scalar2=None,
        op0=ALU.is_lt), (vS, 1))
    if PHASE_LIMIT < 1.3:
        w("SY", dS)
        p2.close()
        a2.close()
        a1.close()
        aA.close()
        finish()
        return
    rows_from_tm(lambda tt: mask_tm[:, tt:tt + 1], mrow)
    if PHASE_LIMIT < 1.4:
        w("SY", dS)
        p2.close()
        a2.close()
        a1.close()
        aA.close()
        finish()
        return
    w("V", vS)
    e("V", lambda g: nc.vector.memset(scan0[0:1, 0:1025], 0.0))
    e("V", lambda g: nc.vector.memset(scan1[0:1, 0:1024], 0.0))
    e("V", lambda g: nc.vector.tensor_copy(out=scan0[0:1, 1025:3072], in_=mrow[0:1, 0:S - 1]))
    cur, nxt = scan0, scan1
    d = 1
    while d < S:
        e("V", (lambda c=cur, n=nxt, d=d: lambda g: nc.vector.tensor_add(
            n[0:1, 1024:3072], c[0:1, 1024:3072], c[0:1, 1024 - d:3072 - d]))())
        cur, nxt = nxt, cur
        d *= 2
    e("V", (lambda c=cur: lambda g: nc.vector.tensor_copy(
        out=prow[0:1, :], in_=c[0:1, 1024:3072]))(), (vS, 1))
    if PHASE_LIMIT < 1.5:
        w("SY", dS)
        p2.close()
        a2.close()
        a1.close()
        aA.close()
        finish()
        return
    w("T", vS)
    for tt in range(NT):
        e("T", (lambda tt=tt: lambda g: nc.tensor.matmul(
            ps_tm16[:, tt:tt + 1], prow[0:1, tt * 128:(tt + 1) * 128],
            onesr_f[0:1, 0:1], start=True, stop=True))(),
          *(((tS, 1),) if tt == NT - 1 else ()))
    w("V", tS)
    e("V", lambda g: nc.vector.tensor_copy(out=pref_tm[:], in_=ps_tm16[:]), (vS, 1))
    if PHASE_LIMIT < 1.6:
        w("SY", dS)
        p2.close()
        a2.close()
        a1.close()
        aA.close()
        finish()
        return
    w("V", vS)
    e("V", lambda g: nc.vector.scalar_tensor_tensor(
        out=scan0[0:1, 0:S], in0=mrow[0:1, :], scalar=1e9, in1=lrow[0:1, :],
        op0=ALU.mult, op1=ALU.add))
    e("V", lambda g: nc.vector.tensor_scalar_add(scan0[0:1, 0:S], scan0[0:1, 0:S], -1e9))
    e("V", lambda g: nc.vector.reduce_max(sm1[0:1, 0:1], scan0[0:1, 0:S], axis=AX))
    e("V", lambda g: nc.vector.tensor_scalar_mul(sm1[0:1, 1:2], sm1[0:1, 0:1], -1.0), (vS, 1))
    w("Sc", vS)
    e("Sc", lambda g: nc.scalar.activation(
        out=scan1[0:1, 0:S], in_=scan0[0:1, 0:S], func=AF.Exp, bias=sm1[0:1, 1:2],
        scale=1.0), (sS, 1))
    w("V", sS)
    e("V", lambda g: nc.vector.reduce_sum(sm1[0:1, 2:3], scan1[0:1, 0:S], axis=AX))
    e("V", lambda g: nc.vector.reciprocal(sm1[0:1, 3:4], sm1[0:1, 2:3]))
    e("V", lambda g: nc.vector.tensor_scalar(
        out=wrow[0:1, :], in0=scan1[0:1, 0:S], scalar1=sm1[0:1, 3:4], scalar2=None,
        op0=ALU.mult), (vS, 1))
    replicate(prow, prep)
    replicate(wrow, wrep)
    if PHASE_LIMIT < 1.7:
        w("SY", dS)
        p2.close()
        a2.close()
        a1.close()
        aA.close()
        finish()
        return
    w("V", vS)
    for tt in range(NT):
        e("V", (lambda tt=tt: lambda g: nc.vector.tensor_scalar(
            out=sel_tm[:, tt * 256:(tt + 1) * 256], in0=iotar[:],
            scalar1=pref_tm[:, tt:tt + 1], scalar2=mask_tm[:, tt:tt + 1],
            op0=ALU.is_equal, op1=ALU.mult))())
    if PHASE_LIMIT < 1.75:
        w("SY", dS)
        p2.close()
        a2.close()
        a1.close()
        aA.close()
        finish()
        return
    for sc in range(2):
        e("V", (lambda sc=sc: lambda g: nc.vector.tensor_scalar(
            out=lrep[:, :], in0=prep[:, :], scalar1=iotac[:, 0:1],
            scalar2=float(sc * 128), op0=ALU.subtract, op1=ALU.subtract))())
        e("V", lambda g: nc.vector.tensor_scalar(
            out=lrep[:, :], in0=lrep[:, :], scalar1=0.0, scalar2=None, op0=ALU.is_equal))
        e("V", (lambda sc=sc: lambda g: nc.vector.tensor_tensor(
            out=selwt[:, sc * S:(sc + 1) * S], in0=lrep[:, :], in1=wrep[:, :],
            op=ALU.mult))())
    if PHASE_LIMIT < 1.8:
        w("SY", dS)
        p2.close()
        a2.close()
        a1.close()
        aA.close()
        finish()
        return
    for nm, src in (("t_mask", mask_tm), ("t_pref", pref_tm), ("t_wrow", wrow),
                    ("t_seltm", sel_tm), ("t_selwt", selwt)):
        if nm in taps:
            w("SY", vS)
            e("SY", (lambda nm=nm, src=src: lambda g: g.dma_start(
                out=taps[nm], in_=src[:]))(), (dS, 16))
    p2.close()
    a2.close()
    a1.close()
    if PHASE_LIMIT < 1.95:
        w("SY", dS)
        aA.close()
        finish()
        return
    barrier()
    if PHASE_LIMIT < 3:
        w("SY", dS)
        aA.close()
        finish()
        return

    a3 = ExitStack()
    xnre = [a3.enter_context(nc.sbuf_tensor(f"xnre{i}", [128, D], BF16)) for i in range(2)]
    fxtm = a3.enter_context(nc.sbuf_tensor("fxtm", [128, 2 * D], BF16))
    fxT = a3.enter_context(nc.sbuf_tensor("fxT", [128, NC * 256], BF16))
    hT = a3.enter_context(nc.sbuf_tensor("hT", [128, NFT * 256], BF16))
    w1b = [a3.enter_context(nc.sbuf_tensor(f"w1b{i}", [128, NC * 512], BF16)) for i in range(2)]
    w2b = [a3.enter_context(nc.sbuf_tensor(f"w2b{i}", [128, 2 * D], BF16)) for i in range(2)]
    w2f = a3.enter_context(nc.sbuf_tensor("w2f", [128, D], F32))
    b2rep = a3.enter_context(nc.sbuf_tensor("b2rep", [128, D], F32))
    e("SY", lambda g: g.dma_start(out=b2rep[:], in_=b2_h.partition_broadcast(128)), (dS, 16))

    p3a = ExitStack()
    ps_fx = p3a.enter_context(nc.psum_tensor("ps_fx", [128, 2 * D], F32))
    gather_t = []
    for tc in range(NT):
        b = tc % 2
        if tc >= 2:
            w("SY", tS, gather_t[tc - 2])
        e("SY", (lambda tc=tc, b=b: lambda g: g.dma_start(
            out=xnre[b][:], in_=xn_sp[tc * 128:(tc + 1) * 128, :]))(), (dS, 16))
        w("T", dS)
        for sh in range(2):
            for dt in range(4):
                last = (sh == 1 and dt == 3)
                e("T", (lambda tc=tc, b=b, sh=sh, dt=dt: lambda g: nc.tensor.matmul(
                    ps_fx[:, sh * D + dt * 512:sh * D + (dt + 1) * 512],
                    sel_tm[:, tc * 256 + sh * 128:tc * 256 + sh * 128 + 128],
                    xnre[b][:, dt * 512:(dt + 1) * 512],
                    start=(tc == 0), stop=(tc == NT - 1), skip_group_check=True))(),
                  *( ((tS, 1),) if last else () ))
        gather_t.append(tS["n"])
    w("V", tS)
    e("V", lambda g: nc.vector.tensor_copy(out=fxtm[:, 0:D], in_=ps_fx[:, 0:D]))
    e("V", lambda g: nc.vector.tensor_copy(out=fxtm[:, D:2 * D], in_=ps_fx[:, D:2 * D]), (vS, 1))
    p3a.close()

    p3b = ExitStack()
    ps_tp = [p3b.enter_context(nc.psum_tensor(f"ps_tp{i}", [128, 256], BF16)) for i in range(2)]
    ps_h = [p3b.enter_context(nc.psum_tensor(f"ps_h{i}", [128, 256], F32)) for i in range(2)]
    fxtp_v = [vS["n"], vS["n"]]
    w("T", vS)
    for dc in range(NC):
        pb = dc % 2
        if dc >= 2:
            w("T", vS, fxtp_v[pb])
        for sh in range(2):
            e("T", (lambda dc=dc, sh=sh, pb=pb: lambda g: nc.tensor.transpose(
                ps_tp[pb][:, sh * 128:sh * 128 + 128],
                fxtm[:, sh * D + dc * 128:sh * D + (dc + 1) * 128], ident_bf[:]))(),
              *( ((tS, 1),) if sh == 1 else () ))
        w("V", tS)
        e("V", (lambda dc=dc, pb=pb: lambda g: nc.vector.tensor_copy(
            out=fxT[:, dc * 256:(dc + 1) * 256], in_=ps_tp[pb][:]))(), (vS, 1))
        fxtp_v[pb] = vS["n"]
    fxT_ready_v = vS["n"]

    hT_ready_s = [0] * NFT
    w1_cons_t = [0, 0]
    ps_h_free_s = [0, 0]
    for ft in range(NFT):
        grp, gb = ft // 4, (ft // 4) % 2
        if ft % 4 == 0:
            if grp >= 2:
                w("G", tS, w1_cons_t[gb])
            e("G", (lambda grp=grp, gb=gb: lambda g: g.dma_start(
                out=w1b[gb][:].rearrange("p (c j) -> p c j", j=512),
                in_=w1_h[:, grp * 512:(grp + 1) * 512]
                .rearrange("(c p) j -> p c j", p=128)))(), (dG, 16))
        pb = ft % 2
        w("T", dG)
        if ft >= 2:
            w("T", sS, ps_h_free_s[pb])
        if ft == 0:
            w("T", vS, fxT_ready_v)
        for dc in range(NC):
            e("T", (lambda ft=ft, gb=gb, pb=pb, dc=dc: lambda g: nc.tensor.matmul(
                ps_h[pb][:, :],
                w1b[gb][:, dc * 512 + (ft % 4) * 128:dc * 512 + (ft % 4) * 128 + 128],
                fxT[:, dc * 256:(dc + 1) * 256],
                start=(dc == 0), stop=(dc == NC - 1), skip_group_check=True))(),
              *( ((tS, 1),) if dc == NC - 1 else () ))
        w1_cons_t[gb] = tS["n"]
        w("Sc", tS)
        e("Sc", (lambda ft=ft, pb=pb: lambda g: nc.scalar.activation(
            out=hT[:, ft * 256:(ft + 1) * 256], in_=ps_h[pb][:], func=AF.Gelu,
            bias=b1_sb[:, ft:ft + 1], scale=1.0))(), (sS, 1))
        hT_ready_s[ft] = sS["n"]
        ps_h_free_s[pb] = sS["n"]
    p3b.close()

    p3c = ExitStack()
    ps_ol = p3c.enter_context(nc.psum_tensor("ps_ol", [128, 2 * D], F32))
    w2_cons_t = [0, 0]
    w2_cast_v = 0
    w("T", sS)
    for grp in range(NFT // 2):
        b = grp % 2
        gate_t = w2_cons_t[b] if grp >= 2 else 0
        w("G", tS, gate_t)
        e("G", (lambda grp=grp, b=b: lambda g: g.dma_start(
            out=w2b[b][:, 0:D], in_=w2_h[(grp * 2) * 128:(grp * 2 + 1) * 128, :]))(),
          (dG, 16))
        gld = dG["n"]
        fc1 = grp * 2 + 1
        if grp >= 1:
            w("SY", vS, w2_cast_v)
        e("SY", (lambda fc1=fc1: lambda g: g.dma_start(
            out=w2f[:], in_=w2_h[fc1 * 128:(fc1 + 1) * 128, :]))(), (dS, 16))
        ldf = dS["n"]
        w("V", tS, gate_t)
        w("V", dS, ldf)
        e("V", (lambda b=b: lambda g: nc.vector.tensor_copy(
            out=w2b[b][:, D:2 * D], in_=w2f[:]))(), (vS, 1))
        w2_cast_v = vS["n"]
        castv = vS["n"]
        for ci in range(2):
            fc = grp * 2 + ci
            w("T", dG, gld)
            w("T", vS, castv)
            w("T", sS, hT_ready_s[fc])
            for sh in range(2):
                for dt in range(4):
                    last = (sh == 1 and dt == 3)
                    e("T", (lambda fc=fc, b=b, ci=ci, sh=sh, dt=dt: lambda g: nc.tensor.matmul(
                        ps_ol[:, sh * D + dt * 512:sh * D + (dt + 1) * 512],
                        hT[:, fc * 256 + sh * 128:fc * 256 + sh * 128 + 128],
                        w2b[b][:, ci * D + dt * 512:ci * D + (dt + 1) * 512],
                        start=(fc == 0), stop=(fc == NFT - 1), skip_group_check=True))(),
                      *( ((tS, 1),) if last else () ))
        w2_cons_t[b] = tS["n"]
    w("V", tS)
    for sh in range(2):
        e("V", (lambda sh=sh: lambda g: nc.vector.tensor_add(
            outl[:, sh * D:(sh + 1) * D], ps_ol[:, sh * D:(sh + 1) * D], b2rep[:]))(),
          *( ((vS, 1),) if sh == 1 else () ))
    if "t_outl" in taps:
        w("SY", vS)
        e("SY", lambda g: g.dma_start(out=taps["t_outl"], in_=outl[:]), (dS, 16))
    p3c.close()
    a3.close()
    aA.close()
    barrier()
    if PHASE_LIMIT < 4:
        w("SY", dS)
        finish()
        return

    aB = ExitStack()
    xnT = aB.enter_context(nc.sbuf_tensor("xnT", [128, NC * S], BF16))
    qtb = [aB.enter_context(nc.sbuf_tensor(f"qtb{i}", [128, S], BF16)) for i in range(2)]
    ktb = [aB.enter_context(nc.sbuf_tensor(f"ktb{i}", [128, S], BF16)) for i in range(2)]
    vtbT = aB.enter_context(nc.sbuf_tensor("vtbT", [128, S], BF16))
    vpb = [aB.enter_context(nc.sbuf_tensor(f"vpb{i}", [128, NT * 130], BF16)) for i in range(2)]
    atb = [aB.enter_context(nc.sbuf_tensor(f"atb{i}", [128, 4 * 512], BF16)) for i in range(2)]
    tdel = [[aB.enter_context(nc.sbuf_tensor(f"tdel{i}{j}", [128, 256], F32))
             for j in range(2)] for i in range(2)]
    ota = aB.enter_context(nc.sbuf_tensor("ota", [65, S], F32))
    otb2 = aB.enter_context(nc.sbuf_tensor("otb2", [65, S], F32))
    rho = aB.enter_context(nc.sbuf_tensor("rho", [1, S], F32))
    rho_b = aB.enter_context(nc.sbuf_tensor("rho_b", [1, S], BF16))
    nrm = [aB.enter_context(nc.sbuf_tensor(f"nrm{i}", [64, S], BF16)) for i in range(2)]
    wpr = [[aB.enter_context(nc.sbuf_tensor(f"wpr{i}{j}", [128, NC * 128], BF16))
            for j in range(3)] for i in range(2)]
    pB = ExitStack()
    psc = [[pB.enter_context(nc.psum_tensor(f"psc{h}{i}", [128, 512], F32))
            for i in range(2)] for h in range(2)]
    po = [pB.enter_context(nc.psum_tensor(f"po{h}", [65, 512], F32)) for h in range(2)]
    ppj = pB.enter_context(nc.psum_tensor("ppj", [128, 512], F32))
    ptp = pB.enter_context(nc.psum_tensor("ptp", [128, 128], BF16))

    w("SY", dS, xn_spill_tgt)
    for dc in range(NC):
        e("SY", (lambda dc=dc: lambda g: g.dma_start(
            out=xnT[:, dc * S:(dc + 1) * S], in_=xn_sp[:, dc * 128:(dc + 1) * 128],
            transpose=True))(), (dS, 16))
    xnT_ready_d = dS["n"]

    whs = (wq_h, wk_h, wv_h)
    wload_d = [0, 0]
    wfree_t = [0, 0]
    qk_ready_v = [0, 0]
    qk_free_t = [0, 0]
    sc_free_s = [[0, 0], [0, 0]]
    ring_free_t = [[0] * 4, [0] * 4]
    po_free_v = vS["n"]
    tdel_free_v = [0, 0]
    norm_busy = {"v": 0, "t": 0}

    pstate = {"ppj_free": 0, "ptp_free": 0, "vstage": 0}

    def load_wpair(p):
        par = p % 2
        if p >= 2:
            w("G", tS, wfree_t[par])
        for wi in range(3):
            e("G", (lambda p=p, par=par, wi=wi: lambda g: g.dma_start(
                out=wpr[par][wi][:].rearrange("p (c j) -> p c j", j=128),
                in_=whs[wi][:, p * 128:(p + 1) * 128]
                .rearrange("(c p) j -> p c j", p=128)))(), (dG, 16))
        wload_d[par] = dG["n"]

    def proj_bundles(p):
        par = p % 2
        items = []

        def mm_half(wi, tn, half, dst):
            def go(wi=wi, tn=tn, half=half, dst=dst):
                w("T", dG, wload_d[par])
                if half == 0 and tn == 0 and wi == 0:
                    w("SY", dS, xnT_ready_d)
                    w("T", dS, xnT_ready_d)
                if half == 0:
                    w("T", vS, pstate["ppj_free"])
                for k in range(8):
                    dc = half * 8 + k
                    e("T", (lambda wi=wi, tn=tn, dc=dc: lambda g: nc.tensor.matmul(
                        ppj[:, :], wpr[par][wi][:, dc * 128:(dc + 1) * 128],
                        xnT[:, dc * S + tn * 512:dc * S + (tn + 1) * 512],
                        start=(dc == 0), stop=(dc == NC - 1), skip_group_check=True))(),
                      *( ((tS, 1),) if dc == NC - 1 else () ))
                if half == 1:
                    w("V", tS)
                    if dst is not None:
                        e("V", (lambda dst=dst, tn=tn: lambda g: nc.vector.tensor_copy(
                            out=dst[:, tn * 512:(tn + 1) * 512], in_=ppj[:, :]))(), (vS, 1))
                    else:
                        e("V", (lambda tn=tn: lambda g: nc.vector.tensor_copy(
                            out=vtbT[:, tn * 512:(tn + 1) * 512], in_=ppj[:, :]))(), (vS, 1))
                    pstate["ppj_free"] = vS["n"]
                    if dst is None:
                        pstate["vstage"] = vS["n"]
            return go

        for wi, dst in ((0, qtb[par]), (1, ktb[par]), (2, None)):
            for tn in range(4):
                items.append(mm_half(wi, tn, 0, dst))
                items.append(mm_half(wi, tn, 1, dst))

        def vtrans(ksub4):
            def go(ksub4=ksub4):
                for k in range(4):
                    ks = ksub4 * 4 + k
                    w("T", vS, max(pstate["vstage"], pstate["ptp_free"]))
                    e("T", (lambda ks=ks: lambda g: nc.tensor.transpose(
                        ptp[:, :], vtbT[:, ks * 128:(ks + 1) * 128], ident_bf[:]))(),
                      (tS, 1))
                    w("V", tS)
                    e("V", (lambda ks=ks, par=par: lambda g: nc.vector.tensor_copy(
                        out=vpb[par][:, ks * 130:ks * 130 + 130]
                        .rearrange("p (h j) -> p h j", j=65)[:, :, 0:64],
                        in_=ptp[:].rearrange("p (h j) -> p h j", j=64)))(), (vS, 1))
                    pstate["ptp_free"] = vS["n"]
            return go

        for k4 in range(4):
            items.append(vtrans(k4))

        def ones_cols():
            e("V", (lambda par=par: lambda g: nc.vector.memset(
                vpb[par][:].rearrange("p (c h j) -> p c h j", h=2, j=65)[:, :, :, 64:65],
                1.0))(), (vS, 1))
            qk_ready_v[par] = vS["n"]
        items.append(ones_cols)
        return items

    load_wpair(0)
    load_wpair(1)
    for it in proj_bundles(0):
        it()
    wfree_t[0] = tS["n"]

    for p in range(NPAIR):
        par = p % 2
        hA, hB = 2 * p, 2 * p + 1
        bg = proj_bundles(p + 1) if p + 1 < NPAIR else []
        if p + 2 < NPAIR:
            load_wpair(p + 2)
        for j, hh in ((0, hA), (1, hB)):
            w("SY", vS, tdel_free_v[par])
            e("SY", (lambda par=par, j=j, hh=hh: lambda g: g.dma_start(
                out=tdel[par][j][:], in_=traw_h[hh]))(), (dS, 16))
        tdel_d = dS["n"]
        w("V", dS, tdel_d)
        for j, hh in ((0, hA), (1, hB)):
            e("V", (lambda par=par, j=j, hh=hh: lambda g: nc.vector.tensor_scalar(
                out=tdel[par][j][:], in0=tdel[par][j][:], scalar1=v0c[:, hh:hh + 1],
                scalar2=-CEXP, op0=ALU.subtract, op1=ALU.add))(),
              *( ((vS, 1),) if j == 1 else () ))
        tdel_ready_v = vS["n"]

        bg_i = 0
        pend_attnv = []

        def push_attnv(go):
            pend_attnv.append(go)
            while len(pend_attnv) > 2:
                pend_attnv.pop(0)()

        def flush_attnv():
            while pend_attnv:
                pend_attnv.pop(0)()

        for qq in range(4):
            qlo = qq * 512
            for ksub in range(NT):
                spar = ksub % 2
                ring = ksub % 4
                kst = ksub * 128
                if qq == 0 and ksub == 0:
                    w("T", vS, qk_ready_v[par])
                    w("T", vS, norm_busy["v"])
                w("T", sS, max(sc_free_s[0][spar], sc_free_s[1][spar]))
                for h in range(2):
                    e("T", (lambda h=h, par=par, spar=spar, kst=kst, qlo=qlo:
                            lambda g: nc.tensor.matmul(
                        psc[h][spar][:, :],
                        ktb[par][h * 64:(h + 1) * 64, kst:kst + 128],
                        qtb[par][h * 64:(h + 1) * 64, qlo:qlo + 512],
                        start=True, stop=True, skip_group_check=True,
                        tile_position=(h * 64, 0)))(),
                      *( ((tS, 1),) if h == 1 else () ))
                sc_t = tS["n"]
                blo, bhi = max(kst, qlo), min(kst + 256, qlo + 512)
                has_band = blo < bhi
                if has_band:
                    w("V", tS, sc_t)
                    if qq == 0 and ksub <= 1:
                        w("V", vS, tdel_ready_v)
                    for h in range(2):
                        e("V", (lambda h=h, par=par, spar=spar, blo=blo, bhi=bhi,
                                qlo=qlo, kst=kst: lambda g: nc.vector.tensor_tensor(
                            out=psc[h][spar][:, blo - qlo:bhi - qlo],
                            in0=psc[h][spar][:, blo - qlo:bhi - qlo],
                            in1=tdel[par][h][:, blo - kst:bhi - kst], op=ALU.add))(),
                          *( ((vS, 1),) if h == 1 else () ))
                    band_v = vS["n"]
                X = kst + 256
                for h, hh in ((0, hA), (1, hB)):
                    w("Sc", tS, sc_t)
                    if has_band:
                        w("Sc", vS, band_v)
                    w("Sc", tS, ring_free_t[h][ring])
                    segs = []
                    l0, h0 = qlo, min(X, qlo + 512)
                    if h0 > l0:
                        segs.append((l0, h0, v0c))
                    l1, h1 = max(X, qlo), qlo + 512
                    if h1 > l1:
                        segs.append((l1, h1, v31c))
                    for si, (sl, sh_, bias) in enumerate(segs):
                        e("Sc", (lambda h=h, hh=hh, spar=spar, ring=ring, sl=sl,
                                 sh_=sh_, qlo=qlo, bias=bias:
                                 lambda g: nc.scalar.activation(
                            out=atb[h][:, ring * 512 + sl - qlo:ring * 512 + sh_ - qlo],
                            in_=psc[h][spar][:, sl - qlo:sh_ - qlo], func=AF.Exp,
                            bias=bias[:, hh:hh + 1], scale=1.0))(),
                          *( ((sS, 1),) if si == len(segs) - 1 else () ))
                    sc_free_s[h][spar] = sS["n"]
                exp_s = sS["n"]

                def mk_attnv(par=par, ksub=ksub, ring=ring, exp_s=exp_s,
                             pfv=(po_free_v if ksub == 0 else None)):
                    def go():
                        w("T", sS, exp_s)
                        if pfv is not None:
                            w("T", vS, pfv)
                        for h in range(2):
                            e("T", (lambda h=h: lambda g: nc.tensor.matmul(
                                po[h][:, :],
                                vpb[par][:, ksub * 130 + h * 65:ksub * 130 + h * 65 + 65],
                                atb[h][:, ring * 512:(ring + 1) * 512],
                                start=(ksub == 0), stop=(ksub == NT - 1),
                                skip_group_check=True))(),
                              *( ((tS, 1),) if h == 1 else () ))
                        ring_free_t[0][ring] = tS["n"]
                        ring_free_t[1][ring] = tS["n"]
                    return go
                push_attnv(mk_attnv())
                if bg_i < len(bg) and ksub % 2 == 1:
                    bg[bg_i]()
                    bg_i += 1
            flush_attnv()
            attnv_t = tS["n"]
            w("V", tS, attnv_t)
            e("V", (lambda qlo=qlo: lambda g: nc.vector.tensor_copy(
                out=ota[:, qlo:qlo + 512], in_=po[0][:]))())
            e("V", (lambda qlo=qlo: lambda g: nc.vector.tensor_copy(
                out=otb2[:, qlo:qlo + 512], in_=po[1][:]))(), (vS, 1))
            po_free_v = vS["n"]
        qk_free_t[par] = tS["n"]
        while bg_i < len(bg):
            bg[bg_i]()
            bg_i += 1
        wfree_t[(p + 1) % 2] = tS["n"]
        tdel_free_v[par] = vS["n"]
        for h, (ot, hh) in enumerate(((ota, hA), (otb2, hB))):
            w("Sc", vS, po_free_v)
            w("Sc", tS, norm_busy["t"])
            e("Sc", (lambda ot=ot: lambda g: nc.scalar.activation(
                out=rho[0:1, :], in_=ot[64:65, :], func=AF.Ln))())
            e("Sc", lambda g: nc.scalar.activation(
                out=rho_b[0:1, :], in_=rho[0:1, :], func=AF.Exp, scale=-1.0), (sS, 1))
            rho_s = sS["n"]
            w("T", sS, rho_s)
            w("T", sS, max(sc_free_s[0][0], sc_free_s[0][1],
                           sc_free_s[1][0], sc_free_s[1][1]))
            for j in range(4):
                e("T", (lambda j=j: lambda g: nc.tensor.matmul(
                    psc[j // 2][j % 2][0:64, :], onesr_b[0:1, 0:64],
                    rho_b[0:1, j * 512:(j + 1) * 512],
                    start=True, stop=True, skip_group_check=True))(),
                  *( ((tS, 1),) if j == 3 else () ))
            w("V", tS)
            for j in range(4):
                e("V", (lambda h=h, j=j, ot=ot: lambda g: nc.vector.tensor_tensor(
                    out=nrm[h][0:64, j * 512:(j + 1) * 512],
                    in0=ot[0:64, j * 512:(j + 1) * 512], in1=psc[j // 2][j % 2][0:64, :],
                    op=ALU.mult))(), *( ((vS, 1),) if j == 3 else () ))
            for hh2 in range(2):
                for pp2 in range(2):
                    sc_free_s[hh2][pp2] = max(sc_free_s[hh2][pp2], sS["n"])
            nb_v = vS["n"]
            norm_busy["t"] = tS["n"]
            w("SY", vS, nb_v)
            e("SY", (lambda h=h, hh=hh: lambda g: g.dma_start(
                out=ot_sp[hh * 64:(hh + 1) * 64, :], in_=nrm[h][0:64, :]))(), (dS, 16))
        norm_busy["v"] = vS["n"]
        po_free_v = vS["n"]
    pB.close()
    aB.close()
    barrier()
    if PHASE_LIMIT < 5:
        w("SY", dS)
        finish()
        return

    aC = ExitStack()
    otre = aC.enter_context(nc.sbuf_tensor("otre", [128, NC * S], BF16))
    wob = aC.enter_context(nc.sbuf_tensor("wob", [128, NC * 1024], BF16))
    xnr5 = [aC.enter_context(nc.sbuf_tensor(f"xnr5{i}", [128, 1024], BF16)) for i in range(2)]
    xr5 = [aC.enter_context(nc.sbuf_tensor(f"xr5{i}", [128, 1024], F32)) for i in range(2)]
    gsb = aC.enter_context(nc.sbuf_tensor("gsb", [128, 1024], F32))
    gs2 = aC.enter_context(nc.sbuf_tensor("gs2", [128, 1024], F32))
    t1b = aC.enter_context(nc.sbuf_tensor("t1b", [128, 1024], F32))
    ob = [aC.enter_context(nc.sbuf_tensor(f"ob{i}", [128, 1024], F32)) for i in range(2)]
    pC = ExitStack()
    ps_s = [pC.enter_context(nc.psum_tensor(f"ps_s{i}", [128, 1024], F32)) for i in range(2)]
    ps_g = [pC.enter_context(nc.psum_tensor(f"ps_g{i}", [128, 1024], F32)) for i in range(2)]

    for dc in range(NC):
        e("SY", (lambda dc=dc: lambda g: g.dma_start(
            out=otre[:, dc * S:(dc + 1) * S], in_=ot_sp[dc * 128:(dc + 1) * 128, :]))(),
          (dS, 16))
    otre_d = dS["n"]
    wob_free_t = 0
    epi_free_v = [0, 0]
    ldone_v = [0, 0]
    for dh in range(2):
        w("G", tS, wob_free_t)
        e("G", (lambda dh=dh: lambda g: g.dma_start(
            out=wob[:].rearrange("p (c j) -> p c j", j=1024),
            in_=wo_h[:, dh * 1024:(dh + 1) * 1024].rearrange("(c p) j -> p c j", p=128)))(),
          (dG, 16))
        wob_d = dG["n"]
        for qt in range(NT):
            pb = qt % 2
            w("SY", vS, ldone_v[pb])
            e("SY", (lambda dh=dh, qt=qt, pb=pb: lambda g: g.dma_start(
                out=xnr5[pb][:], in_=xn_sp[qt * 128:(qt + 1) * 128,
                                          dh * 1024:(dh + 1) * 1024]))(), (dS, 16))
            e("SY", (lambda dh=dh, qt=qt, pb=pb: lambda g: g.dma_start(
                out=xr5[pb][:], in_=x_h[qt * 128:(qt + 1) * 128,
                                        dh * 1024:(dh + 1) * 1024]))(), (dS, 16))
            ld_d = dS["n"]
            w("T", dG, wob_d)
            if dh == 0 and qt == 0:
                w("SY", dS, otre_d)
                w("T", dS, otre_d)
            w("T", vS, epi_free_v[pb])
            for dc in range(NC):
                for dt in range(2):
                    last = (dc == NC - 1 and dt == 1)
                    e("T", (lambda dc=dc, dt=dt, qt=qt, pb=pb: lambda g: nc.tensor.matmul(
                        ps_s[pb][:, dt * 512:(dt + 1) * 512],
                        otre[:, dc * S + qt * 128:dc * S + (qt + 1) * 128],
                        wob[:, dc * 1024 + dt * 512:dc * 1024 + (dt + 1) * 512],
                        start=(dc == 0), stop=(dc == NC - 1), skip_group_check=True))(),
                      *( ((tS, 1),) if last else () ))
            s_t = tS["n"]
            for sc in range(2):
                for dt in range(2):
                    last = (sc == 1 and dt == 1)
                    e("T", (lambda sc=sc, dt=dt, dh=dh, qt=qt, pb=pb:
                            lambda g: nc.tensor.matmul(
                        ps_g[pb][:, dt * 512:(dt + 1) * 512],
                        selwt[:, sc * S + qt * 128:sc * S + (qt + 1) * 128],
                        outl[:, sc * D + dh * 1024 + dt * 512:sc * D + dh * 1024 + (dt + 1) * 512],
                        start=(sc == 0), stop=(sc == 1), skip_group_check=True))(),
                      *( ((tS, 1),) if last else () ))
            g_t = tS["n"]
            w("V", tS, g_t)
            w("V", dS, ld_d)
            e("V", (lambda pb=pb: lambda g: nc.vector.tensor_add(
                gsb[:], ps_g[pb][:], xnr5[pb][:]))(), (vS, 1))
            w("Sc", vS)
            e("Sc", lambda g: nc.scalar.activation(
                out=gs2[:], in_=gsb[:], func=AF.Silu), (sS, 1))
            w("V", sS)
            e("V", (lambda pb=pb: lambda g: nc.vector.tensor_mul(
                t1b[:], ps_s[pb][:], gs2[:]))())
            e("V", (lambda pb=pb: lambda g: nc.vector.tensor_add(
                ob[pb][:], t1b[:], xr5[pb][:]))(), (vS, 1))
            epi_free_v[pb] = vS["n"]
            ldone_v[pb] = vS["n"]
            w("SY", vS, epi_free_v[pb])
            e("SY", (lambda dh=dh, qt=qt, pb=pb: lambda g: g.dma_start(
                out=out_h[qt * 128:(qt + 1) * 128, dh * 1024:(dh + 1) * 1024],
                in_=ob[pb][:]))(), (dS, 16))
        wob_free_t = tS["n"]
    w("SY", dS)
    pC.close()
    aC.close()

    w("G", dS, 0)
    finish()


def kernel(L0_attn_hidden_states, L0_attn_Wq, L0_attn_Wk, L0_attn_Wv, L0_attn_Wo,
           L0_attn_rel_bias, L0_ffn_Wr, L0_ffn_br, L0_ffn_W1, L0_ffn_b1,
           L0_ffn_W2, L0_ffn_b2):
    X = np.ascontiguousarray(np.asarray(L0_attn_hidden_states, np.float32))
    consts = host_consts(L0_attn_rel_bias, L0_ffn_Wr, L0_ffn_br, L0_ffn_b1, L0_ffn_b2)
    shared = {
        "Wq": np.ascontiguousarray(np.asarray(L0_attn_Wq, np.float32)),
        "Wk": np.ascontiguousarray(np.asarray(L0_attn_Wk, np.float32)),
        "Wv": np.ascontiguousarray(np.asarray(L0_attn_Wv, np.float32)),
        "Wo": np.ascontiguousarray(np.asarray(L0_attn_Wo, np.float32)),
        "W1": np.ascontiguousarray(np.asarray(L0_ffn_W1, np.float32)),
        "W2": np.ascontiguousarray(np.asarray(L0_ffn_W2, np.float32)),
        **consts,
    }
    nc = bass.Bass("TRN2", target_bir_lowering=False, detect_race_conditions=False)
    build(nc)
    in_maps = [{"x": np.ascontiguousarray(X[b]), **shared} for b in range(8)]
    res = run_bass_kernel_spmd(nc, in_maps, core_ids=list(range(8)))
    out = np.stack([np.asarray(res.results[b]["out"], np.float32) for b in range(8)])
    return out


# version 21
# speedup vs baseline: 1.0795x; 1.0795x over previous
import math
from contextlib import ExitStack

import numpy as np

import concourse.bass as bass
import concourse.mybir as mybir
from concourse.bass_utils import run_bass_kernel_spmd

F32 = mybir.dt.float32
BF16 = mybir.dt.bfloat16
AF = mybir.ActivationFunctionType
ALU = mybir.AluOpType
AX = mybir.AxisListType.X

S = 2048
D = 2048
DFF = 8192
NH = 32
NSEL = 256
CEXP = 30.0
EPS = 1e-5
NT = S // 128
NC = D // 128
NFT = DFF // 128
NPAIR = NH // 2

DEV_TAPS = ()
PHASE_LIMIT = 99


def _bucket(d):
    rp = max(d, 0)
    if rp < 16:
        return rp
    b = 16 + int(math.log(rp / 16.0) / math.log(8.0) * 16.0)
    return min(b, 31)


def host_consts(rel_bias, Wr, br, b1, b2):
    rel = np.asarray(rel_bias, np.float32)
    bidx = np.empty((128, 256), np.int64)
    for p in range(128):
        for c in range(256):
            bidx[p, c] = _bucket(c - p)
    traw = np.ascontiguousarray(rel[bidx].transpose(2, 0, 1))
    return {
        "traw": traw,
        "v0r": np.ascontiguousarray(np.broadcast_to(rel[0], (128, NH))),
        "v31r": np.ascontiguousarray(np.broadcast_to(rel[31], (128, NH))),
        "ident": np.eye(128, dtype=np.float32),
        "iota_r": np.arange(256, dtype=np.float32).reshape(1, 256),
        "iota_c": np.arange(128, dtype=np.float32).reshape(128, 1),
        "eyef16": np.ascontiguousarray(np.eye(16, dtype=np.float32).reshape(1, 256)),
        "ones_r": np.ones((1, 128), np.float32),
        "wr_row": np.ascontiguousarray(np.asarray(Wr, np.float32).reshape(1, D)),
        "br_in": np.asarray(br, np.float32).reshape(1, 1),
        "b1_in": np.ascontiguousarray(np.asarray(b1, np.float32).reshape(DFF // 128, 128).T),
        "b2_in": np.ascontiguousarray(np.asarray(b2, np.float32).reshape(1, D)),
    }


def build(nc):
    es = ExitStack()
    dev = set(DEV_TAPS)

    x_h = nc.dram_tensor("x", [S, D], F32, kind="ExternalInput").ap()
    wq_h = nc.dram_tensor("Wq", [D, D], F32, kind="ExternalInput").ap()
    wk_h = nc.dram_tensor("Wk", [D, D], F32, kind="ExternalInput").ap()
    wv_h = nc.dram_tensor("Wv", [D, D], F32, kind="ExternalInput").ap()
    wo_h = nc.dram_tensor("Wo", [D, D], F32, kind="ExternalInput").ap()
    w1_h = nc.dram_tensor("W1", [D, DFF], F32, kind="ExternalInput").ap()
    w2_h = nc.dram_tensor("W2", [DFF, D], F32, kind="ExternalInput").ap()
    traw_h = nc.dram_tensor("traw", [NH, 128, 256], F32, kind="ExternalInput").ap()
    v0r_h = nc.dram_tensor("v0r", [128, NH], F32, kind="ExternalInput").ap()
    v31r_h = nc.dram_tensor("v31r", [128, NH], F32, kind="ExternalInput").ap()
    ident_h = nc.dram_tensor("ident", [128, 128], F32, kind="ExternalInput").ap()
    iota_r_h = nc.dram_tensor("iota_r", [1, 256], F32, kind="ExternalInput").ap()
    iota_c_h = nc.dram_tensor("iota_c", [128, 1], F32, kind="ExternalInput").ap()
    eyef16_h = nc.dram_tensor("eyef16", [1, 256], F32, kind="ExternalInput").ap()
    ones_r_h = nc.dram_tensor("ones_r", [1, 128], F32, kind="ExternalInput").ap()
    wr_row_h = nc.dram_tensor("wr_row", [1, D], F32, kind="ExternalInput").ap()
    br_h = nc.dram_tensor("br_in", [1, 1], F32, kind="ExternalInput").ap()
    b1_h = nc.dram_tensor("b1_in", [128, DFF // 128], F32, kind="ExternalInput").ap()
    b2_h = nc.dram_tensor("b2_in", [1, D], F32, kind="ExternalInput").ap()
    out_h = nc.dram_tensor("out", [S, D], F32, kind="ExternalOutput").ap()

    spill_kind = "ExternalOutput" if "spills" in dev else "Internal"
    xn_sp = nc.dram_tensor("xn_sp", [S, D], BF16, kind=spill_kind).ap()
    ot_sp = nc.dram_tensor("ot_sp", [D, S], BF16, kind=spill_kind).ap()
    taps = {}
    for name, shape, dt in (
        ("t_ltm", [128, NT], F32), ("t_mask", [128, NT], F32),
        ("t_pref", [128, NT], F32), ("t_wrow", [1, S], F32),
        ("t_seltm", [128, NT * 256], BF16), ("t_selwt", [128, 2 * S], BF16),
        ("t_outl", [128, 2 * D], BF16), ("t_stat", [128, NT * 8], F32),
    ):
        if name in dev:
            taps[name] = nc.dram_tensor(name, shape, dt, kind="ExternalOutput").ap()

    plan = {k: [] for k in ("T", "V", "Sc", "SY", "G")}

    def mksem(name):
        return {"h": es.enter_context(nc.semaphore(name)), "n": 0}

    tS, vS, sS = (mksem(n) for n in ("tS", "vS", "sS"))

    class DQ:
        def __init__(self, sems):
            self.sems = sems
            self.i = 0
            self.log = []

        def __getitem__(self, k):
            assert k == "n"
            return 16 * len(self.log)

    dS = DQ([mksem(f"dS{i}") for i in range(8)])
    dG = DQ([mksem(f"dG{i}") for i in range(4)])

    def w(st, sem, tgt=None):
        if isinstance(sem, DQ):
            k = len(sem.log) if tgt is None else (tgt + 15) // 16
            last = {}
            for s, cnt in sem.log[:k]:
                last[id(s)] = (s, cnt)
            for s, cnt in last.values():
                plan[st].append(("w", s, cnt))
            return
        plan[st].append(("w", sem, sem["n"] if tgt is None else tgt))

    OWN = {"V": vS, "Sc": sS}

    def e_nw(st, fn, *incs):
        own = OWN.get(st)
        realincs = [(own, 1)] if own is not None else []
        for sem, amt in incs:
            sem["n"] += amt
            realincs.append((sem, amt))
        if own is not None:
            own["n"] += 1
        merged = {}
        for s_, a_ in realincs:
            k_ = id(s_)
            merged[k_] = (s_, merged[k_][1] + a_) if k_ in merged else (s_, a_)
        plan[st].append(("e", fn, list(merged.values())))

    def e(st, fn, *incs):
        realincs = []
        own = OWN.get(st)
        if own is not None:
            plan[st].append(("w", own, own["n"]))
            own["n"] += 1
            realincs.append((own, 1))
        for sem, amt in incs:
            if isinstance(sem, DQ):
                s = sem.sems[sem.i]
                sem.i = (sem.i + 1) % len(sem.sems)
                plan[st].append(("w", s, s["n"]))
                sem.log.append((s, s["n"] + 16))
                s["n"] += 16
                realincs.append((s, 16))
            else:
                sem["n"] += amt
                realincs.append((sem, amt))
        merged = {}
        for s_, a_ in realincs:
            k_ = id(s_)
            if k_ in merged:
                merged[k_] = (s_, merged[k_][1] + a_)
            else:
                merged[k_] = (s_, a_)
        plan[st].append(("e", fn, list(merged.values())))

    ENG = {"T": lambda: nc.tensor, "V": lambda: nc.vector, "Sc": lambda: nc.scalar}

    def barrier():
        marks = {"T": tS["n"], "V": vS["n"], "Sc": sS["n"]}
        for st in ("T", "V", "Sc", "SY", "G"):
            for st2 in ("T", "V", "Sc"):
                if st2 != st:
                    w(st, {"T": tS, "V": vS, "Sc": sS}[st2], marks[st2])
            w(st, dS)
            w(st, dG)

    def finish():
        def runner(st):
            def go(eng):
                for item in plan[st]:
                    if item[0] == "w":
                        eng.wait_ge(item[1]["h"], item[2])
                    else:
                        inst = item[1](eng)
                        for sem, amt in item[2]:
                            inst = inst.then_inc(sem["h"], amt)
            return go
        with nc.Block() as block:
            block.tensor(runner("T"))
            block.vector(runner("V"))
            block.scalar(runner("Sc"))
            block.sync(runner("SY"))
            block.gpsimd(runner("G"))
        es.close()

    selwt = es.enter_context(nc.sbuf_tensor("selwt", [128, 2 * S], BF16))
    outl = es.enter_context(nc.sbuf_tensor("outl", [128, 2 * D], BF16))
    ident_bf = es.enter_context(nc.sbuf_tensor("ident_bf", [128, 128], BF16))
    ident_f = es.enter_context(nc.sbuf_tensor("ident_f", [128, 128], F32))
    v0c = es.enter_context(nc.sbuf_tensor("v0c", [128, NH], F32))
    v31c = es.enter_context(nc.sbuf_tensor("v31c", [128, NH], F32))
    onesr_f = es.enter_context(nc.sbuf_tensor("onesr_f", [1, 128], F32))
    onesr_b = es.enter_context(nc.sbuf_tensor("onesr_b", [1, 128], BF16))

    e("G", lambda g: g.dma_start(out=ident_bf[:], in_=ident_h), (dG, 16))
    e("G", lambda g: g.dma_start(out=onesr_b[:], in_=ones_r_h), (dG, 16))
    e("SY", lambda g: g.dma_start(out=ident_f[:], in_=ident_h), (dS, 16))
    e("SY", lambda g: g.dma_start(out=v0c[:], in_=v0r_h), (dS, 16))
    e("SY", lambda g: g.dma_start(out=v31c[:], in_=v31r_h), (dS, 16))
    e("SY", lambda g: g.dma_start(out=onesr_f[:], in_=ones_r_h), (dS, 16))
    w("V", dS)
    e("V", lambda g: nc.vector.tensor_scalar_add(v0c[:], v0c[:], -CEXP))
    e("V", lambda g: nc.vector.tensor_scalar_add(v31c[:], v31c[:], -CEXP), (vS, 1))

    aA = ExitStack()
    sel_tm = aA.enter_context(nc.sbuf_tensor("sel_tm", [128, NT * 256], BF16))
    iotar = aA.enter_context(nc.sbuf_tensor("iotar", [128, 256], F32))
    iotac = aA.enter_context(nc.sbuf_tensor("iotac", [128, 1], F32))
    eyef16 = aA.enter_context(nc.sbuf_tensor("eyef16_s", [1, 256], F32))
    b1_sb = aA.enter_context(nc.sbuf_tensor("b1_sb", [128, NFT], F32))
    brr = aA.enter_context(nc.sbuf_tensor("brr", [128, 1], F32))
    ltm = aA.enter_context(nc.sbuf_tensor("ltm", [128, NT], F32))
    rank_tm = aA.enter_context(nc.sbuf_tensor("rank_tm", [128, NT], F32))
    mask_tm = aA.enter_context(nc.sbuf_tensor("mask_tm", [128, NT], F32))
    pref_tm = aA.enter_context(nc.sbuf_tensor("pref_tm", [128, NT], F32))
    lrow = aA.enter_context(nc.sbuf_tensor("lrow", [1, S], F32))
    mrow = aA.enter_context(nc.sbuf_tensor("mrow", [1, S], F32))
    prow = aA.enter_context(nc.sbuf_tensor("prow", [1, S], F32))
    wrow = aA.enter_context(nc.sbuf_tensor("wrow", [1, S], F32))
    sm1 = aA.enter_context(nc.sbuf_tensor("sm1", [1, 8], F32))
    scrb = aA.enter_context(nc.sbuf_tensor("scrb", [128, D], BF16))

    e("SY", lambda g: g.dma_start(out=iotar[:], in_=iota_r_h.partition_broadcast(128)), (dS, 16))
    e("SY", lambda g: g.dma_start(out=iotac[:], in_=iota_c_h), (dS, 16))
    e("SY", lambda g: g.dma_start(out=eyef16[:], in_=eyef16_h), (dS, 16))
    e("SY", lambda g: g.dma_start(out=b1_sb[:], in_=b1_h), (dS, 16))
    e("SY", lambda g: g.dma_start(out=brr[:], in_=br_h.partition_broadcast(128)), (dS, 16))

    a1 = ExitStack()
    xbuf = [a1.enter_context(nc.sbuf_tensor(f"xbuf{i}", [128, D], F32)) for i in range(2)]
    xnb = [a1.enter_context(nc.sbuf_tensor(f"xnb{i}", [128, D], BF16)) for i in range(2)]
    xnf = a1.enter_context(nc.sbuf_tensor("xnf", [128, D], F32))
    wrr = a1.enter_context(nc.sbuf_tensor("wrr", [128, D], F32))
    stat = a1.enter_context(nc.sbuf_tensor("stat", [128, 8], F32))
    statall = a1.enter_context(nc.sbuf_tensor("statall", [128, NT * 8], F32)) if "t_stat" in taps else None

    e("SY", lambda g: g.dma_start(out=wrr[:], in_=wr_row_h.partition_broadcast(128)), (dS, 16))
    for tt in range(NT):
        b = tt % 2
        if tt >= 2:
            w("SY", vS)
        e("SY", (lambda tt=tt, b=b: lambda g: g.dma_start(
            out=xbuf[b][:], in_=x_h[tt * 128:(tt + 1) * 128, :]))(), (dS, 16))
        w("V", dS)
        e("V", (lambda b=b: lambda g: nc.vector.reduce_sum(stat[:, 4:5], xbuf[b][:], axis=AX))())
        e("V", lambda g: nc.vector.tensor_scalar_mul(stat[:, 0:1], stat[:, 4:5], -1.0 / D), (vS, 1))
        w("Sc", vS)
        e("Sc", (lambda b=b: lambda g: nc.scalar.activation(
            out=scrb[:], in_=xbuf[b][:], func=AF.Square, bias=stat[:, 0:1], scale=1.0,
            accum_out=stat[:, 1:2]))(), (sS, 1))
        w("V", sS)
        e("V", lambda g: nc.vector.tensor_scalar(
            out=stat[:, 2:3], in0=stat[:, 1:2], scalar1=1.0 / D, scalar2=EPS,
            op0=ALU.mult, op1=ALU.add))
        e("V", lambda g: nc.vector.reciprocal(stat[:, 4:5], stat[:, 2:3]), (vS, 1))
        w("Sc", vS)
        e("Sc", lambda g: nc.scalar.activation(
            out=stat[:, 5:6], in_=stat[:, 4:5], func=AF.Sqrt), (sS, 1))
        w("V", sS)
        e("V", lambda g: nc.vector.tensor_mul(stat[:, 6:7], stat[:, 5:6], stat[:, 5:6]))
        e("V", lambda g: nc.vector.tensor_mul(stat[:, 6:7], stat[:, 6:7], stat[:, 2:3]))
        e("V", lambda g: nc.vector.tensor_scalar(
            out=stat[:, 6:7], in0=stat[:, 6:7], scalar1=-0.5, scalar2=1.5,
            op0=ALU.mult, op1=ALU.add))
        e("V", lambda g: nc.vector.tensor_mul(stat[:, 3:4], stat[:, 5:6], stat[:, 6:7]))
        e("V", (lambda b=b: lambda g: nc.vector.tensor_scalar(
            out=xnf[:], in0=xbuf[b][:], scalar1=stat[:, 0:1], scalar2=stat[:, 3:4],
            op0=ALU.add, op1=ALU.mult))())
        e("V", (lambda b=b: lambda g: nc.vector.tensor_copy(out=xnb[b][:], in_=xnf[:]))())
        e("V", (lambda tt=tt: lambda g: nc.vector.scalar_tensor_tensor(
            out=scrb[:], in0=xnf[:], scalar=1.0, in1=wrr[:], op0=ALU.mult, op1=ALU.mult,
            accum_out=ltm[:, tt:tt + 1]))(), (vS, 1))
        if statall is not None:
            e("V", (lambda tt=tt: lambda g: nc.vector.tensor_copy(
                out=statall[:, tt * 8:(tt + 1) * 8], in_=stat[:]))(), (vS, 1))
        w("SY", vS)
        e("SY", (lambda tt=tt, b=b: lambda g: g.dma_start(
            out=xn_sp[tt * 128:(tt + 1) * 128, :], in_=xnb[b][:]))(), (dS, 16))
    xn_spill_tgt = dS["n"]
    e("V", lambda g: nc.vector.tensor_scalar(
        out=ltm[:], in0=ltm[:], scalar1=brr[:, 0:1], scalar2=None, op0=ALU.add), (vS, 1))
    if "t_ltm" in taps:
        w("SY", vS)
        e("SY", lambda g: g.dma_start(out=taps["t_ltm"], in_=ltm[:]), (dS, 16))
    if statall is not None:
        w("SY", vS)
        e("SY", lambda g: g.dma_start(out=taps["t_stat"], in_=statall[:]), (dS, 16))
    if PHASE_LIMIT < 2:
        w("SY", dS)
        a1.close()
        aA.close()
        finish()
        return

    a2 = ExitStack()
    lrep = a2.enter_context(nc.sbuf_tensor("lrep", [128, S], F32))
    prep = a2.enter_context(nc.sbuf_tensor("prep", [128, S], F32))
    wrep = a2.enter_context(nc.sbuf_tensor("wrep", [128, S], F32))
    scan0 = a2.enter_context(nc.sbuf_tensor("scan0", [1, 3072], F32))
    scan1 = a2.enter_context(nc.sbuf_tensor("scan1", [1, 3072], F32))
    p2 = ExitStack()
    ps_big = p2.enter_context(nc.psum_tensor("ps_big", [128, S], F32))
    ps_tm16 = p2.enter_context(nc.psum_tensor("ps_tm16", [128, NT], F32))

    def rows_from_tm(col_fn, dst_row):
        w("T", vS)
        for tt in range(NT):
            e("T", (lambda tt=tt: lambda g: nc.tensor.matmul(
                ps_big[0:1, tt * 128:(tt + 1) * 128], col_fn(tt), ident_f[:],
                start=True, stop=True))(), *(((tS, 1),) if tt == NT - 1 else ()))
        w("V", tS)
        e("V", (lambda d_=dst_row: lambda g: nc.vector.tensor_copy(
            out=d_[0:1, :], in_=ps_big[0:1, :]))(), (vS, 1))

    def replicate(row_sb, dst_rep):
        w("T", vS)
        for j in range(4):
            e("T", (lambda j=j, r=row_sb: lambda g: nc.tensor.matmul(
                ps_big[:, j * 512:(j + 1) * 512], onesr_f[0:1, :],
                r[0:1, j * 512:(j + 1) * 512], start=True, stop=True))(),
              *(((tS, 1),) if j == 3 else ()))
        w("V", tS)
        e("V", (lambda d_=dst_rep: lambda g: nc.vector.tensor_copy(
            out=d_[:], in_=ps_big[:]))(), (vS, 1))

    w("T", dS)
    rows_from_tm(lambda tt: ltm[:, tt:tt + 1], lrow)
    replicate(lrow, lrep)
    if PHASE_LIMIT < 1.2:
        w("SY", dS)
        p2.close()
        a2.close()
        a1.close()
        aA.close()
        finish()
        return
    w("V", vS)
    for tt in range(NT):
        e("V", (lambda tt=tt: lambda g: nc.vector.tensor_scalar(
            out=scrb[:, 0:S], in0=lrep[:], scalar1=ltm[:, tt:tt + 1], scalar2=None,
            op0=ALU.is_gt, op1=ALU.add, accum_out=rank_tm[:, tt:tt + 1]))())
    e("V", lambda g: nc.vector.tensor_scalar(
        out=mask_tm[:], in0=rank_tm[:], scalar1=float(NSEL), scalar2=None,
        op0=ALU.is_lt), (vS, 1))
    if PHASE_LIMIT < 1.3:
        w("SY", dS)
        p2.close()
        a2.close()
        a1.close()
        aA.close()
        finish()
        return
    rows_from_tm(lambda tt: mask_tm[:, tt:tt + 1], mrow)
    if PHASE_LIMIT < 1.4:
        w("SY", dS)
        p2.close()
        a2.close()
        a1.close()
        aA.close()
        finish()
        return
    w("V", vS)
    e("V", lambda g: nc.vector.memset(scan0[0:1, 0:1025], 0.0))
    e("V", lambda g: nc.vector.memset(scan1[0:1, 0:1024], 0.0))
    e("V", lambda g: nc.vector.tensor_copy(out=scan0[0:1, 1025:3072], in_=mrow[0:1, 0:S - 1]))
    cur, nxt = scan0, scan1
    d = 1
    while d < S:
        e("V", (lambda c=cur, n=nxt, d=d: lambda g: nc.vector.tensor_add(
            n[0:1, 1024:3072], c[0:1, 1024:3072], c[0:1, 1024 - d:3072 - d]))())
        cur, nxt = nxt, cur
        d *= 2
    e("V", (lambda c=cur: lambda g: nc.vector.tensor_copy(
        out=prow[0:1, :], in_=c[0:1, 1024:3072]))(), (vS, 1))
    if PHASE_LIMIT < 1.5:
        w("SY", dS)
        p2.close()
        a2.close()
        a1.close()
        aA.close()
        finish()
        return
    w("T", vS)
    for tt in range(NT):
        e("T", (lambda tt=tt: lambda g: nc.tensor.matmul(
            ps_tm16[:, tt:tt + 1], prow[0:1, tt * 128:(tt + 1) * 128],
            onesr_f[0:1, 0:1], start=True, stop=True))(),
          *(((tS, 1),) if tt == NT - 1 else ()))
    w("V", tS)
    e("V", lambda g: nc.vector.tensor_copy(out=pref_tm[:], in_=ps_tm16[:]), (vS, 1))
    if PHASE_LIMIT < 1.6:
        w("SY", dS)
        p2.close()
        a2.close()
        a1.close()
        aA.close()
        finish()
        return
    w("V", vS)
    e("V", lambda g: nc.vector.scalar_tensor_tensor(
        out=scan0[0:1, 0:S], in0=mrow[0:1, :], scalar=1e9, in1=lrow[0:1, :],
        op0=ALU.mult, op1=ALU.add))
    e("V", lambda g: nc.vector.tensor_scalar_add(scan0[0:1, 0:S], scan0[0:1, 0:S], -1e9))
    e("V", lambda g: nc.vector.reduce_max(sm1[0:1, 0:1], scan0[0:1, 0:S], axis=AX))
    e("V", lambda g: nc.vector.tensor_scalar_mul(sm1[0:1, 1:2], sm1[0:1, 0:1], -1.0), (vS, 1))
    w("Sc", vS)
    e("Sc", lambda g: nc.scalar.activation(
        out=scan1[0:1, 0:S], in_=scan0[0:1, 0:S], func=AF.Exp, bias=sm1[0:1, 1:2],
        scale=1.0), (sS, 1))
    w("V", sS)
    e("V", lambda g: nc.vector.reduce_sum(sm1[0:1, 2:3], scan1[0:1, 0:S], axis=AX))
    e("V", lambda g: nc.vector.reciprocal(sm1[0:1, 3:4], sm1[0:1, 2:3]))
    e("V", lambda g: nc.vector.tensor_scalar(
        out=wrow[0:1, :], in0=scan1[0:1, 0:S], scalar1=sm1[0:1, 3:4], scalar2=None,
        op0=ALU.mult), (vS, 1))
    replicate(prow, prep)
    replicate(wrow, wrep)
    if PHASE_LIMIT < 1.7:
        w("SY", dS)
        p2.close()
        a2.close()
        a1.close()
        aA.close()
        finish()
        return
    w("V", vS)
    for tt in range(NT):
        e("V", (lambda tt=tt: lambda g: nc.vector.tensor_scalar(
            out=sel_tm[:, tt * 256:(tt + 1) * 256], in0=iotar[:],
            scalar1=pref_tm[:, tt:tt + 1], scalar2=mask_tm[:, tt:tt + 1],
            op0=ALU.is_equal, op1=ALU.mult))())
    if PHASE_LIMIT < 1.75:
        w("SY", dS)
        p2.close()
        a2.close()
        a1.close()
        aA.close()
        finish()
        return
    for sc in range(2):
        e("V", (lambda sc=sc: lambda g: nc.vector.tensor_scalar(
            out=lrep[:, :], in0=prep[:, :], scalar1=iotac[:, 0:1],
            scalar2=float(sc * 128), op0=ALU.subtract, op1=ALU.subtract))())
        e("V", lambda g: nc.vector.tensor_scalar(
            out=lrep[:, :], in0=lrep[:, :], scalar1=0.0, scalar2=None, op0=ALU.is_equal))
        e("V", (lambda sc=sc: lambda g: nc.vector.tensor_tensor(
            out=selwt[:, sc * S:(sc + 1) * S], in0=lrep[:, :], in1=wrep[:, :],
            op=ALU.mult))())
    if PHASE_LIMIT < 1.8:
        w("SY", dS)
        p2.close()
        a2.close()
        a1.close()
        aA.close()
        finish()
        return
    for nm, src in (("t_mask", mask_tm), ("t_pref", pref_tm), ("t_wrow", wrow),
                    ("t_seltm", sel_tm), ("t_selwt", selwt)):
        if nm in taps:
            w("SY", vS)
            e("SY", (lambda nm=nm, src=src: lambda g: g.dma_start(
                out=taps[nm], in_=src[:]))(), (dS, 16))
    p2.close()
    a2.close()
    a1.close()
    if PHASE_LIMIT < 1.95:
        w("SY", dS)
        aA.close()
        finish()
        return
    barrier()
    if PHASE_LIMIT < 3:
        w("SY", dS)
        aA.close()
        finish()
        return

    a3 = ExitStack()
    xnre = [a3.enter_context(nc.sbuf_tensor(f"xnre{i}", [128, D], BF16)) for i in range(2)]
    fxtm = a3.enter_context(nc.sbuf_tensor("fxtm", [128, 2 * D], BF16))
    fxT = a3.enter_context(nc.sbuf_tensor("fxT", [128, NC * 256], BF16))
    hT = a3.enter_context(nc.sbuf_tensor("hT", [128, NFT * 256], BF16))
    w1b = [a3.enter_context(nc.sbuf_tensor(f"w1b{i}", [128, NC * 512], BF16)) for i in range(2)]
    w2b = [a3.enter_context(nc.sbuf_tensor(f"w2b{i}", [128, 2 * D], BF16)) for i in range(2)]
    b2rep = a3.enter_context(nc.sbuf_tensor("b2rep", [128, D], F32))
    e("SY", lambda g: g.dma_start(out=b2rep[:], in_=b2_h.partition_broadcast(128)), (dS, 16))

    p3a = ExitStack()
    ps_fx = p3a.enter_context(nc.psum_tensor("ps_fx", [128, 2 * D], F32))
    gather_t = []
    for tc in range(NT):
        b = tc % 2
        if tc >= 2:
            w("SY", tS, gather_t[tc - 2])
        e("SY", (lambda tc=tc, b=b: lambda g: g.dma_start(
            out=xnre[b][:], in_=xn_sp[tc * 128:(tc + 1) * 128, :]))(), (dS, 16))
        w("T", dS)
        for sh in range(2):
            for dt in range(4):
                last = (sh == 1 and dt == 3)
                e("T", (lambda tc=tc, b=b, sh=sh, dt=dt: lambda g: nc.tensor.matmul(
                    ps_fx[:, sh * D + dt * 512:sh * D + (dt + 1) * 512],
                    sel_tm[:, tc * 256 + sh * 128:tc * 256 + sh * 128 + 128],
                    xnre[b][:, dt * 512:(dt + 1) * 512],
                    start=(tc == 0), stop=(tc == NT - 1), skip_group_check=True))(),
                  *( ((tS, 1),) if last else () ))
        gather_t.append(tS["n"])
    w("V", tS)
    e("V", lambda g: nc.vector.tensor_copy(out=fxtm[:, 0:D], in_=ps_fx[:, 0:D]))
    e("V", lambda g: nc.vector.tensor_copy(out=fxtm[:, D:2 * D], in_=ps_fx[:, D:2 * D]), (vS, 1))
    p3a.close()

    p3b = ExitStack()
    ps_tp = [p3b.enter_context(nc.psum_tensor(f"ps_tp{i}", [128, 256], BF16)) for i in range(2)]
    ps_h = [p3b.enter_context(nc.psum_tensor(f"ps_h{i}", [128, 256], F32)) for i in range(2)]
    fxtp_v = [vS["n"], vS["n"]]
    w("T", vS)
    for dc in range(NC):
        pb = dc % 2
        if dc >= 2:
            w("T", vS, fxtp_v[pb])
        for sh in range(2):
            e("T", (lambda dc=dc, sh=sh, pb=pb: lambda g: nc.tensor.transpose(
                ps_tp[pb][:, sh * 128:sh * 128 + 128],
                fxtm[:, sh * D + dc * 128:sh * D + (dc + 1) * 128], ident_bf[:]))(),
              *( ((tS, 1),) if sh == 1 else () ))
        w("V", tS)
        e("V", (lambda dc=dc, pb=pb: lambda g: nc.vector.tensor_copy(
            out=fxT[:, dc * 256:(dc + 1) * 256], in_=ps_tp[pb][:]))(), (vS, 1))
        fxtp_v[pb] = vS["n"]
    fxT_ready_v = vS["n"]

    hT_ready_s = [0] * NFT
    w1_cons_t = [0, 0]
    ps_h_free_s = [0, 0]
    for ft in range(NFT):
        grp, gb = ft // 4, (ft // 4) % 2
        if ft % 4 == 0:
            if grp >= 2:
                w("G", tS, w1_cons_t[gb])
            e("G", (lambda grp=grp, gb=gb: lambda g: g.dma_start(
                out=w1b[gb][:].rearrange("p (c j) -> p c j", j=512),
                in_=w1_h[:, grp * 512:(grp + 1) * 512]
                .rearrange("(c p) j -> p c j", p=128)))(), (dG, 16))
        pb = ft % 2
        w("T", dG)
        if ft >= 2:
            w("T", sS, ps_h_free_s[pb])
        if ft == 0:
            w("T", vS, fxT_ready_v)
        for dc in range(NC):
            e("T", (lambda ft=ft, gb=gb, pb=pb, dc=dc: lambda g: nc.tensor.matmul(
                ps_h[pb][:, :],
                w1b[gb][:, dc * 512 + (ft % 4) * 128:dc * 512 + (ft % 4) * 128 + 128],
                fxT[:, dc * 256:(dc + 1) * 256],
                start=(dc == 0), stop=(dc == NC - 1), skip_group_check=True))(),
              *( ((tS, 1),) if dc == NC - 1 else () ))
        w1_cons_t[gb] = tS["n"]
        w("Sc", tS)
        e_nw("Sc", (lambda ft=ft, pb=pb: lambda g: nc.scalar.activation(
            out=hT[:, ft * 256:(ft + 1) * 256], in_=ps_h[pb][:], func=AF.Gelu,
            bias=b1_sb[:, ft:ft + 1], scale=1.0))(), (sS, 1))
        hT_ready_s[ft] = sS["n"]
        ps_h_free_s[pb] = sS["n"]
    p3b.close()

    p3c = ExitStack()
    ps_ol = p3c.enter_context(nc.psum_tensor("ps_ol", [128, 2 * D], F32))
    w2_cons_t = [0, 0]
    w("T", sS)
    for grp in range(NFT // 2):
        b = grp % 2
        if grp >= 2:
            w("G", tS, w2_cons_t[b])
        e("G", (lambda grp=grp, b=b: lambda g: g.dma_start(
            out=w2b[b][:].rearrange("p (c j) -> p c j", j=D),
            in_=w2_h[grp * 256:(grp + 1) * 256, :]
            .rearrange("(c p) j -> p c j", p=128)))(), (dG, 16))
        ld = dG["n"]
        for ci in range(2):
            fc = grp * 2 + ci
            w("T", dG, ld)
            w("T", sS, hT_ready_s[fc])
            for sh in range(2):
                for dt in range(4):
                    last = (sh == 1 and dt == 3)
                    e("T", (lambda fc=fc, b=b, ci=ci, sh=sh, dt=dt: lambda g: nc.tensor.matmul(
                        ps_ol[:, sh * D + dt * 512:sh * D + (dt + 1) * 512],
                        hT[:, fc * 256 + sh * 128:fc * 256 + sh * 128 + 128],
                        w2b[b][:, ci * D + dt * 512:ci * D + (dt + 1) * 512],
                        start=(fc == 0), stop=(fc == NFT - 1), skip_group_check=True))(),
                      *( ((tS, 1),) if last else () ))
        w2_cons_t[b] = tS["n"]
    w("V", tS)
    for sh in range(2):
        e("V", (lambda sh=sh: lambda g: nc.vector.tensor_add(
            outl[:, sh * D:(sh + 1) * D], ps_ol[:, sh * D:(sh + 1) * D], b2rep[:]))(),
          *( ((vS, 1),) if sh == 1 else () ))
    if "t_outl" in taps:
        w("SY", vS)
        e("SY", lambda g: g.dma_start(out=taps["t_outl"], in_=outl[:]), (dS, 16))
    p3c.close()
    a3.close()
    aA.close()
    barrier()
    if PHASE_LIMIT < 4:
        w("SY", dS)
        finish()
        return

    aB = ExitStack()
    xnT = aB.enter_context(nc.sbuf_tensor("xnT", [128, NC * S], BF16))
    qtb = [aB.enter_context(nc.sbuf_tensor(f"qtb{i}", [128, S], BF16)) for i in range(2)]
    ktb = [aB.enter_context(nc.sbuf_tensor(f"ktb{i}", [128, S], BF16)) for i in range(2)]
    vtbT = aB.enter_context(nc.sbuf_tensor("vtbT", [128, S], BF16))
    vpb = [aB.enter_context(nc.sbuf_tensor(f"vpb{i}", [128, NT * 130], BF16)) for i in range(2)]
    atb = [aB.enter_context(nc.sbuf_tensor(f"atb{i}", [128, 4 * 512], BF16)) for i in range(2)]
    tdel = [[aB.enter_context(nc.sbuf_tensor(f"tdel{i}{j}", [128, 256], F32))
             for j in range(2)] for i in range(2)]
    ota = aB.enter_context(nc.sbuf_tensor("ota", [65, S], F32))
    otb2 = aB.enter_context(nc.sbuf_tensor("otb2", [65, S], F32))
    rho = aB.enter_context(nc.sbuf_tensor("rho", [1, S], F32))
    rho_b = aB.enter_context(nc.sbuf_tensor("rho_b", [1, S], BF16))
    nrm = [aB.enter_context(nc.sbuf_tensor(f"nrm{i}", [64, S], BF16)) for i in range(2)]
    wpr = [[aB.enter_context(nc.sbuf_tensor(f"wpr{i}{j}", [128, NC * 128], BF16))
            for j in range(3)] for i in range(2)]
    pB = ExitStack()
    psc = [[pB.enter_context(nc.psum_tensor(f"psc{h}{i}", [128, 512], F32))
            for i in range(2)] for h in range(2)]
    po = [pB.enter_context(nc.psum_tensor(f"po{h}", [65, 512], F32)) for h in range(2)]
    ppj = pB.enter_context(nc.psum_tensor("ppj", [128, 512], F32))
    ptp = pB.enter_context(nc.psum_tensor("ptp", [128, 128], BF16))

    w("SY", dS, xn_spill_tgt)
    for dc in range(NC):
        e("SY", (lambda dc=dc: lambda g: g.dma_start(
            out=xnT[:, dc * S:(dc + 1) * S], in_=xn_sp[:, dc * 128:(dc + 1) * 128],
            transpose=True))(), (dS, 16))
    xnT_ready_d = dS["n"]

    whs = (wq_h, wk_h, wv_h)
    wload_d = [0, 0]
    wfree_t = [0, 0]
    qk_ready_v = [0, 0]
    qk_free_t = [0, 0]
    sc_free_s = [[0, 0], [0, 0]]
    ring_free_t = [[0] * 4, [0] * 4]
    po_free_v = vS["n"]
    tdel_free_v = [0, 0]
    norm_busy = {"v": 0, "t": 0}

    pstate = {"ppj_free": 0, "ptp_free": 0, "vstage": 0}

    def load_wpair(p):
        par = p % 2
        if p >= 2:
            w("G", tS, wfree_t[par])
        for wi in range(3):
            e("G", (lambda p=p, par=par, wi=wi: lambda g: g.dma_start(
                out=wpr[par][wi][:].rearrange("p (c j) -> p c j", j=128),
                in_=whs[wi][:, p * 128:(p + 1) * 128]
                .rearrange("(c p) j -> p c j", p=128)))(), (dG, 16))
        wload_d[par] = dG["n"]

    def proj_bundles(p):
        par = p % 2
        items = []

        def mm_half(wi, tn, half, dst):
            def go(wi=wi, tn=tn, half=half, dst=dst):
                w("T", dG, wload_d[par])
                if half == 0 and tn == 0 and wi == 0:
                    w("SY", dS, xnT_ready_d)
                    w("T", dS, xnT_ready_d)
                if half == 0:
                    w("T", vS, pstate["ppj_free"])
                for k in range(8):
                    dc = half * 8 + k
                    e("T", (lambda wi=wi, tn=tn, dc=dc: lambda g: nc.tensor.matmul(
                        ppj[:, :], wpr[par][wi][:, dc * 128:(dc + 1) * 128],
                        xnT[:, dc * S + tn * 512:dc * S + (tn + 1) * 512],
                        start=(dc == 0), stop=(dc == NC - 1), skip_group_check=True))(),
                      *( ((tS, 1),) if dc == NC - 1 else () ))
                if half == 1:
                    w("V", tS)
                    if dst is not None:
                        e("V", (lambda dst=dst, tn=tn: lambda g: nc.vector.tensor_copy(
                            out=dst[:, tn * 512:(tn + 1) * 512], in_=ppj[:, :]))(), (vS, 1))
                    else:
                        e("V", (lambda tn=tn: lambda g: nc.vector.tensor_copy(
                            out=vtbT[:, tn * 512:(tn + 1) * 512], in_=ppj[:, :]))(), (vS, 1))
                    pstate["ppj_free"] = vS["n"]
                    if dst is None:
                        pstate["vstage"] = vS["n"]
            return go

        for wi, dst in ((0, qtb[par]), (1, ktb[par]), (2, None)):
            for tn in range(4):
                items.append(mm_half(wi, tn, 0, dst))
                items.append(mm_half(wi, tn, 1, dst))

        def vtrans(ksub4):
            def go(ksub4=ksub4):
                for k in range(4):
                    ks = ksub4 * 4 + k
                    w("T", vS, max(pstate["vstage"], pstate["ptp_free"]))
                    e("T", (lambda ks=ks: lambda g: nc.tensor.transpose(
                        ptp[:, :], vtbT[:, ks * 128:(ks + 1) * 128], ident_bf[:]))(),
                      (tS, 1))
                    w("V", tS)
                    e("V", (lambda ks=ks, par=par: lambda g: nc.vector.tensor_copy(
                        out=vpb[par][:, ks * 130:ks * 130 + 130]
                        .rearrange("p (h j) -> p h j", j=65)[:, :, 0:64],
                        in_=ptp[:].rearrange("p (h j) -> p h j", j=64)))(), (vS, 1))
                    pstate["ptp_free"] = vS["n"]
            return go

        for k4 in range(4):
            items.append(vtrans(k4))

        def ones_cols():
            e("V", (lambda par=par: lambda g: nc.vector.memset(
                vpb[par][:].rearrange("p (c h j) -> p c h j", h=2, j=65)[:, :, :, 64:65],
                1.0))(), (vS, 1))
            qk_ready_v[par] = vS["n"]
        items.append(ones_cols)
        return items

    load_wpair(0)
    load_wpair(1)
    for it in proj_bundles(0):
        it()
    wfree_t[0] = tS["n"]

    for p in range(NPAIR):
        par = p % 2
        hA, hB = 2 * p, 2 * p + 1
        bg = proj_bundles(p + 1) if p + 1 < NPAIR else []
        if p + 2 < NPAIR:
            load_wpair(p + 2)
        for j, hh in ((0, hA), (1, hB)):
            w("SY", vS, tdel_free_v[par])
            e("SY", (lambda par=par, j=j, hh=hh: lambda g: g.dma_start(
                out=tdel[par][j][:], in_=traw_h[hh]))(), (dS, 16))
        tdel_d = dS["n"]
        w("V", dS, tdel_d)
        for j, hh in ((0, hA), (1, hB)):
            e("V", (lambda par=par, j=j, hh=hh: lambda g: nc.vector.tensor_scalar(
                out=tdel[par][j][:], in0=tdel[par][j][:], scalar1=v0c[:, hh:hh + 1],
                scalar2=-CEXP, op0=ALU.subtract, op1=ALU.add))(),
              *( ((vS, 1),) if j == 1 else () ))
        tdel_ready_v = vS["n"]

        bg_i = 0
        pend_attnv = []

        def push_attnv(go):
            pend_attnv.append(go)
            while len(pend_attnv) > 2:
                pend_attnv.pop(0)()

        def flush_attnv():
            while pend_attnv:
                pend_attnv.pop(0)()

        for qq in range(4):
            qlo = qq * 512
            for ksub in range(NT):
                spar = ksub % 2
                ring = ksub % 4
                kst = ksub * 128
                if qq == 0 and ksub == 0:
                    w("T", vS, qk_ready_v[par])
                    w("T", vS, norm_busy["v"])
                w("T", sS, max(sc_free_s[0][spar], sc_free_s[1][spar]))
                for h in range(2):
                    e("T", (lambda h=h, par=par, spar=spar, kst=kst, qlo=qlo:
                            lambda g: nc.tensor.matmul(
                        psc[h][spar][:, :],
                        ktb[par][h * 64:(h + 1) * 64, kst:kst + 128],
                        qtb[par][h * 64:(h + 1) * 64, qlo:qlo + 512],
                        start=True, stop=True, skip_group_check=True,
                        tile_position=(h * 64, 0)))(),
                      *( ((tS, 1),) if h == 1 else () ))
                sc_t = tS["n"]
                blo, bhi = max(kst, qlo), min(kst + 256, qlo + 512)
                has_band = blo < bhi
                if has_band:
                    w("V", tS, sc_t)
                    if qq == 0 and ksub <= 1:
                        w("V", vS, tdel_ready_v)
                    for h in range(2):
                        e("V", (lambda h=h, par=par, spar=spar, blo=blo, bhi=bhi,
                                qlo=qlo, kst=kst: lambda g: nc.vector.tensor_tensor(
                            out=psc[h][spar][:, blo - qlo:bhi - qlo],
                            in0=psc[h][spar][:, blo - qlo:bhi - qlo],
                            in1=tdel[par][h][:, blo - kst:bhi - kst], op=ALU.add))(),
                          *( ((vS, 1),) if h == 1 else () ))
                    band_v = vS["n"]
                X = kst + 256
                for h, hh in ((0, hA), (1, hB)):
                    w("Sc", tS, sc_t)
                    if has_band:
                        w("Sc", vS, band_v)
                    w("Sc", tS, ring_free_t[h][ring])
                    segs = []
                    l0, h0 = qlo, min(X, qlo + 512)
                    if h0 > l0:
                        segs.append((l0, h0, v0c))
                    l1, h1 = max(X, qlo), qlo + 512
                    if h1 > l1:
                        segs.append((l1, h1, v31c))
                    for si, (sl, sh_, bias) in enumerate(segs):
                        e_nw("Sc", (lambda h=h, hh=hh, spar=spar, ring=ring, sl=sl,
                                    sh_=sh_, qlo=qlo, bias=bias:
                                    lambda g: nc.scalar.activation(
                            out=atb[h][:, ring * 512 + sl - qlo:ring * 512 + sh_ - qlo],
                            in_=psc[h][spar][:, sl - qlo:sh_ - qlo], func=AF.Exp,
                            bias=bias[:, hh:hh + 1], scale=1.0))(),
                          *( ((sS, 1),) if si == len(segs) - 1 else () ))
                    sc_free_s[h][spar] = sS["n"]
                exp_s = sS["n"]

                def mk_attnv(par=par, ksub=ksub, ring=ring, exp_s=exp_s,
                             pfv=(po_free_v if ksub == 0 else None)):
                    def go():
                        w("T", sS, exp_s)
                        if pfv is not None:
                            w("T", vS, pfv)
                        for h in range(2):
                            e("T", (lambda h=h: lambda g: nc.tensor.matmul(
                                po[h][:, :],
                                vpb[par][:, ksub * 130 + h * 65:ksub * 130 + h * 65 + 65],
                                atb[h][:, ring * 512:(ring + 1) * 512],
                                start=(ksub == 0), stop=(ksub == NT - 1),
                                skip_group_check=True))(),
                              *( ((tS, 1),) if h == 1 else () ))
                        ring_free_t[0][ring] = tS["n"]
                        ring_free_t[1][ring] = tS["n"]
                    return go
                push_attnv(mk_attnv())
                if bg_i < len(bg) and ksub % 2 == 1:
                    bg[bg_i]()
                    bg_i += 1
            flush_attnv()
            attnv_t = tS["n"]
            w("V", tS, attnv_t)
            e("V", (lambda qlo=qlo: lambda g: nc.vector.tensor_copy(
                out=ota[:, qlo:qlo + 512], in_=po[0][:]))())
            e("V", (lambda qlo=qlo: lambda g: nc.vector.tensor_copy(
                out=otb2[:, qlo:qlo + 512], in_=po[1][:]))(), (vS, 1))
            po_free_v = vS["n"]
        qk_free_t[par] = tS["n"]
        while bg_i < len(bg):
            bg[bg_i]()
            bg_i += 1
        wfree_t[(p + 1) % 2] = tS["n"]
        tdel_free_v[par] = vS["n"]
        for h, (ot, hh) in enumerate(((ota, hA), (otb2, hB))):
            w("Sc", vS, po_free_v)
            w("Sc", tS, norm_busy["t"])
            e("Sc", (lambda ot=ot: lambda g: nc.scalar.activation(
                out=rho[0:1, :], in_=ot[64:65, :], func=AF.Ln))())
            e("Sc", lambda g: nc.scalar.activation(
                out=rho_b[0:1, :], in_=rho[0:1, :], func=AF.Exp, scale=-1.0), (sS, 1))
            rho_s = sS["n"]
            w("T", sS, rho_s)
            w("T", sS, max(sc_free_s[0][0], sc_free_s[0][1],
                           sc_free_s[1][0], sc_free_s[1][1]))
            for j in range(4):
                e("T", (lambda j=j: lambda g: nc.tensor.matmul(
                    psc[j // 2][j % 2][0:64, :], onesr_b[0:1, 0:64],
                    rho_b[0:1, j * 512:(j + 1) * 512],
                    start=True, stop=True, skip_group_check=True))(),
                  *( ((tS, 1),) if j == 3 else () ))
            w("V", tS)
            for j in range(4):
                e("V", (lambda h=h, j=j, ot=ot: lambda g: nc.vector.tensor_tensor(
                    out=nrm[h][0:64, j * 512:(j + 1) * 512],
                    in0=ot[0:64, j * 512:(j + 1) * 512], in1=psc[j // 2][j % 2][0:64, :],
                    op=ALU.mult))(), *( ((vS, 1),) if j == 3 else () ))
            for hh2 in range(2):
                for pp2 in range(2):
                    sc_free_s[hh2][pp2] = max(sc_free_s[hh2][pp2], sS["n"])
            nb_v = vS["n"]
            norm_busy["t"] = tS["n"]
            w("SY", vS, nb_v)
            e("SY", (lambda h=h, hh=hh: lambda g: g.dma_start(
                out=ot_sp[hh * 64:(hh + 1) * 64, :], in_=nrm[h][0:64, :]))(), (dS, 16))
        norm_busy["v"] = vS["n"]
        po_free_v = vS["n"]
    pB.close()
    aB.close()
    barrier()
    if PHASE_LIMIT < 5:
        w("SY", dS)
        finish()
        return

    aC = ExitStack()
    otre = aC.enter_context(nc.sbuf_tensor("otre", [128, NC * S], BF16))
    wob = aC.enter_context(nc.sbuf_tensor("wob", [128, NC * 1024], BF16))
    xnr5 = [aC.enter_context(nc.sbuf_tensor(f"xnr5{i}", [128, 1024], BF16)) for i in range(2)]
    xr5 = [aC.enter_context(nc.sbuf_tensor(f"xr5{i}", [128, 1024], F32)) for i in range(2)]
    gsb = aC.enter_context(nc.sbuf_tensor("gsb", [128, 1024], F32))
    gs2 = aC.enter_context(nc.sbuf_tensor("gs2", [128, 1024], F32))
    t1b = aC.enter_context(nc.sbuf_tensor("t1b", [128, 1024], F32))
    ob = [aC.enter_context(nc.sbuf_tensor(f"ob{i}", [128, 1024], F32)) for i in range(2)]
    pC = ExitStack()
    ps_s = [pC.enter_context(nc.psum_tensor(f"ps_s{i}", [128, 1024], F32)) for i in range(2)]
    ps_g = [pC.enter_context(nc.psum_tensor(f"ps_g{i}", [128, 1024], F32)) for i in range(2)]

    for dc in range(NC):
        e("SY", (lambda dc=dc: lambda g: g.dma_start(
            out=otre[:, dc * S:(dc + 1) * S], in_=ot_sp[dc * 128:(dc + 1) * 128, :]))(),
          (dS, 16))
    otre_d = dS["n"]
    wob_free_t = 0
    epi_free_v = [0, 0]
    ldone_v = [0, 0]
    for dh in range(2):
        w("G", tS, wob_free_t)
        e("G", (lambda dh=dh: lambda g: g.dma_start(
            out=wob[:].rearrange("p (c j) -> p c j", j=1024),
            in_=wo_h[:, dh * 1024:(dh + 1) * 1024].rearrange("(c p) j -> p c j", p=128)))(),
          (dG, 16))
        wob_d = dG["n"]
        for qt in range(NT):
            pb = qt % 2
            w("SY", vS, ldone_v[pb])
            e("SY", (lambda dh=dh, qt=qt, pb=pb: lambda g: g.dma_start(
                out=xnr5[pb][:], in_=xn_sp[qt * 128:(qt + 1) * 128,
                                          dh * 1024:(dh + 1) * 1024]))(), (dS, 16))
            e("SY", (lambda dh=dh, qt=qt, pb=pb: lambda g: g.dma_start(
                out=xr5[pb][:], in_=x_h[qt * 128:(qt + 1) * 128,
                                        dh * 1024:(dh + 1) * 1024]))(), (dS, 16))
            ld_d = dS["n"]
            w("T", dG, wob_d)
            if dh == 0 and qt == 0:
                w("SY", dS, otre_d)
                w("T", dS, otre_d)
            w("T", vS, epi_free_v[pb])
            for dc in range(NC):
                for dt in range(2):
                    last = (dc == NC - 1 and dt == 1)
                    e("T", (lambda dc=dc, dt=dt, qt=qt, pb=pb: lambda g: nc.tensor.matmul(
                        ps_s[pb][:, dt * 512:(dt + 1) * 512],
                        otre[:, dc * S + qt * 128:dc * S + (qt + 1) * 128],
                        wob[:, dc * 1024 + dt * 512:dc * 1024 + (dt + 1) * 512],
                        start=(dc == 0), stop=(dc == NC - 1), skip_group_check=True))(),
                      *( ((tS, 1),) if last else () ))
            s_t = tS["n"]
            for sc in range(2):
                for dt in range(2):
                    last = (sc == 1 and dt == 1)
                    e("T", (lambda sc=sc, dt=dt, dh=dh, qt=qt, pb=pb:
                            lambda g: nc.tensor.matmul(
                        ps_g[pb][:, dt * 512:(dt + 1) * 512],
                        selwt[:, sc * S + qt * 128:sc * S + (qt + 1) * 128],
                        outl[:, sc * D + dh * 1024 + dt * 512:sc * D + dh * 1024 + (dt + 1) * 512],
                        start=(sc == 0), stop=(sc == 1), skip_group_check=True))(),
                      *( ((tS, 1),) if last else () ))
            g_t = tS["n"]
            w("V", tS, g_t)
            w("V", dS, ld_d)
            e("V", (lambda pb=pb: lambda g: nc.vector.tensor_add(
                gsb[:], ps_g[pb][:], xnr5[pb][:]))(), (vS, 1))
            w("Sc", vS)
            e("Sc", lambda g: nc.scalar.activation(
                out=gs2[:], in_=gsb[:], func=AF.Silu), (sS, 1))
            w("V", sS)
            e("V", (lambda pb=pb: lambda g: nc.vector.tensor_mul(
                t1b[:], ps_s[pb][:], gs2[:]))())
            e("V", (lambda pb=pb: lambda g: nc.vector.tensor_add(
                ob[pb][:], t1b[:], xr5[pb][:]))(), (vS, 1))
            epi_free_v[pb] = vS["n"]
            ldone_v[pb] = vS["n"]
            w("SY", vS, epi_free_v[pb])
            e("SY", (lambda dh=dh, qt=qt, pb=pb: lambda g: g.dma_start(
                out=out_h[qt * 128:(qt + 1) * 128, dh * 1024:(dh + 1) * 1024],
                in_=ob[pb][:]))(), (dS, 16))
        wob_free_t = tS["n"]
    w("SY", dS)
    pC.close()
    aC.close()

    w("G", dS, 0)
    finish()


def kernel(L0_attn_hidden_states, L0_attn_Wq, L0_attn_Wk, L0_attn_Wv, L0_attn_Wo,
           L0_attn_rel_bias, L0_ffn_Wr, L0_ffn_br, L0_ffn_W1, L0_ffn_b1,
           L0_ffn_W2, L0_ffn_b2):
    X = np.ascontiguousarray(np.asarray(L0_attn_hidden_states, np.float32))
    consts = host_consts(L0_attn_rel_bias, L0_ffn_Wr, L0_ffn_br, L0_ffn_b1, L0_ffn_b2)
    shared = {
        "Wq": np.ascontiguousarray(np.asarray(L0_attn_Wq, np.float32)),
        "Wk": np.ascontiguousarray(np.asarray(L0_attn_Wk, np.float32)),
        "Wv": np.ascontiguousarray(np.asarray(L0_attn_Wv, np.float32)),
        "Wo": np.ascontiguousarray(np.asarray(L0_attn_Wo, np.float32)),
        "W1": np.ascontiguousarray(np.asarray(L0_ffn_W1, np.float32)),
        "W2": np.ascontiguousarray(np.asarray(L0_ffn_W2, np.float32)),
        **consts,
    }
    nc = bass.Bass("TRN2", target_bir_lowering=False, detect_race_conditions=False)
    build(nc)
    in_maps = [{"x": np.ascontiguousarray(X[b]), **shared} for b in range(8)]
    res = run_bass_kernel_spmd(nc, in_maps, core_ids=list(range(8)))
    out = np.stack([np.asarray(res.results[b]["out"], np.float32) for b in range(8)])
    return out
